# version 07
# baseline (speedup 1.0000x reference)
import math
import contextlib
import numpy as np
import ml_dtypes
import concourse.bass as bass
import concourse.mybir as mybir
from concourse.bass_utils import run_bass_kernel_spmd

F32 = mybir.dt.float32
BF16 = mybir.dt.bfloat16
AF = mybir.ActivationFunctionType
ALU = mybir.AluOpType
AX = mybir.AxisListType

D = 2048
TK = 2048
T = 4096
NFC = 16
NTG = 4
HM = 8
INTER = 5632
NIC = 44
SCL = 1.0 / math.sqrt(128.0)
NEG = -1.0e30
N_SUB = 4

ENGINES = ("sync", "vector", "scalar", "tensor", "gpsimd")
GROUPS = [[0, 1], [2, 3], [4, 5], [6, 7]]


class Prog:
    def __init__(self):
        self.ops = {e: [] for e in ENGINES}
        self.cnt = {}

    def val(self, s):
        return self.cnt.get(s, 0)

    def bump(self, s, amt):
        self.cnt[s] = self.cnt.get(s, 0) + amt
        return self.cnt[s]

    def inc(self, eng, s, amt, fn):
        v = self.bump(s, amt)
        self.ops[eng].append(("inc", s, amt, fn))
        return v

    def wait(self, eng, s, v):
        if v > 0:
            self.ops[eng].append(("wait", s, v))

    def raw(self, eng, fn):
        self.ops[eng].append(("raw", fn))


def build_program(n_sub=N_SUB, dbg=False, stop_after=None):
    nc = bass.Bass("TRN2", target_bir_lowering=False, debug=False)

    def din(name, shape, dt=F32):
        return nc.dram_tensor(name, shape, dt, kind="ExternalInput").ap()

    xin = din("xin", [D, TK])
    wa_qkv = din("wa_qkv", [D, 3 * D])
    wa_proj = din("wa_proj", [D, D])
    wb_qkv = din("wb_qkv", [D, 3 * D])
    wb_bias = din("wb_bias", [128, 48])
    wb_o = din("wb_o", [D, D])
    wb_bo = din("wb_bo", [128, 16])
    w_gate = din("w_gate", [D, INTER])
    w_up = din("w_up", [D, INTER])
    w_down = din("w_down", [INTER, D])
    s_gate = din("s_gate", [128, NIC])
    s_up = din("s_up", [128, NIC])
    s_down = din("s_down", [128, 16])
    cos_in = din("cos_in", [128, TK])
    sin_in = din("sin_in", [128, TK])
    rope_in = din("rope_in", [128, 128], BF16)
    identb_in = din("identb_in", [128, 128], BF16)
    identf_in = din("identf_in", [128, 128], F32)
    mask_in = din("mask_in", [128, 128])
    onesc_in = din("onesc_in", [128, 1], BF16)
    onesr_in = din("onesr_in", [1, 128])
    esel_in = din("esel_in", [4, 512])
    sclv_in = din("sclv_in", [128, 1])

    y_out = nc.dram_tensor("y_out", [D, TK], F32, kind="ExternalOutput").ap()
    if dbg:
        dbg_qkv = nc.dram_tensor("dbg_qkv", [3072, T], BF16, kind="ExternalOutput").ap()
        dbg_ao = nc.dram_tensor("dbg_ao", [2048, T], BF16, kind="ExternalOutput").ap()
        dbg_loc = nc.dram_tensor("dbg_loc", [6144, TK], BF16, kind="ExternalOutput").ap()
        dbg_xln = nc.dram_tensor("dbg_xln", [D, TK], BF16, kind="ExternalOutput").ap()
        dbg_xln_r = dbg_xln.rearrange("(c p) t -> p c t", p=128)

    x_str = nc.dram_tensor("x_str", [D, TK], F32).ap()
    qkv_loc = nc.dram_tensor("qkv_loc", [6144, TK], BF16).ap()
    g1 = nc.dram_tensor("g1", [2 * 3072, TK], BF16).ap()
    qkvabs = nc.dram_tensor("qkvabs", [3072, T], BF16).ap()
    ao_loc = nc.dram_tensor("ao_loc", [1024, T], BF16).ap()
    g2 = nc.dram_tensor("g2", [2048, T], BF16).ap()
    h_dram = nc.dram_tensor("h_dram", [INTER, TK], BF16).ap()
    down_bfd = nc.dram_tensor("down_bfd", [INTER, D], BF16).ap()

    xin_r = xin.rearrange("(c p) t -> p c t", p=128)
    xstr_r = x_str.rearrange("(c p) t -> p c t", p=128)
    yout_r = y_out.rearrange("(c p) t -> p c t", p=128)
    g2_r = g2.rearrange("(c p) t -> p c t", p=128)
    h_r = h_dram.rearrange("(c p) t -> p c t", p=128)

    def wview(w):
        return w.rearrange("(c p) m -> p c m", p=128)

    wa_qkv_r, wb_qkv_r = wview(wa_qkv), wview(wb_qkv)
    wa_proj_r, wb_o_r = wview(wa_proj), wview(wb_o)
    w_gate_r, w_up_r = wview(w_gate), wview(w_up)
    wdown_r = wview(w_down)
    wdownbf_r = down_bfd.rearrange("(c p) m -> p c m", p=128)

    prog = Prog()
    SEMS = {}
    REG = {}
    BANKF = {}
    BUFF = {}

    def bank_wait(eng, name):
        for s, v in BANKF.get(name, {}).items():
            prog.wait(eng, s, v)

    def bank_set(name, sem, val):
        BANKF[name] = {sem: val}

    def buf_wait(eng, key):
        for s, v in BUFF.get(key, {}).items():
            prog.wait(eng, s, v)

    def buf_set(key, sem, val):
        BUFF[key] = {sem: val}

    def buf_add(key, sem, val):
        BUFF.setdefault(key, {})[sem] = val

    stack = contextlib.ExitStack()

    def sb(name, shape, dt):
        return stack.enter_context(nc.sbuf_tensor(name, shape, dt))

    def psumt(name, shape, dt):
        return stack.enter_context(nc.psum_tensor(name, shape, dt))

    with stack:
        act_s = sb("act_s", [128, 16, 2048], BF16)
        big2 = sb("big2", [128, 22528], BF16)
        scr16 = sb("scr16", [128, 4096], F32)
        kfm = sb("kfm", [128, T], BF16)
        qg_t = sb("qg_s", [128, 2, 512], BF16)
        qg_s = [qg_t[:, i, :] for i in range(2)]
        vch_t = sb("vchunk", [128, 2, 512], BF16)
        vchunk = [vch_t[:, i, :] for i in range(2)]
        v_tm = sb("v_tm", [128, T], BF16)
        w32_t = sb("w_f32", [128, 2, 8, 128], F32)
        w_f32 = [w32_t[:, i, :, :] for i in range(2)]
        wbf_t = sb("w_bf", [128, 2, 16, 128], BF16)
        w_bf = [wbf_t[:, i, :, :] for i in range(2)]
        xold_t = sb("xold", [128, 4, 512], F32)
        xold = [xold_t[:, i, :] for i in range(4)]
        xnew_t = sb("xnew", [128, 2, 512], F32)
        xnew = [xnew_t[:, i, :] for i in range(2)]
        xbf_t = sb("xbf", [128, 4, 512], BF16)
        xbf = [xbf_t[:, i, :] for i in range(4)]
        xsq_t = sb("xsq", [128, 4, 512], BF16)
        xsq = [xsq_t[:, i, :] for i in range(4)]
        qst_t = sb("qstage", [128, 2, 512], BF16)
        qstage = [qst_t[:, i, :] for i in range(2)]
        rtmp_t = sb("rtmp", [128, 2, 512], F32)
        rtmp = [rtmp_t[:, i, :] for i in range(2)]
        ptT_t = sb("ptT", [128, 2, 512], BF16)
        ptT = [ptT_t[:, i, :] for i in range(2)]
        aost_t = sb("aost", [128, 2, 512], BF16)
        aost = [aost_t[:, i, :] for i in range(2)]
        rlrep = sb("rlrep", [128, 512], F32)
        gst = sb("gst", [128, 2048], BF16)
        hst_t = sb("hst", [128, 2, 512], BF16)
        hst = [hst_t[:, i, :] for i in range(2)]
        nparts_t = sb("nparts", [128, 16], F32)
        nparts = [nparts_t[:, 0:8], nparts_t[:, 8:16]]
        l4 = sb("l4", [128, 4], F32)
        rl4 = sb("rl4", [128, 4], F32)
        rlT = sb("rlT", [4, 128], F32)
        negm_t = sb("negm", [128, 2], F32)
        negm = [negm_t[:, i:i + 1] for i in range(2)]
        negms_t = sb("negms", [128, 2], F32)
        negms = [negms_t[:, i:i + 1] for i in range(2)]
        cosb = big2[:, 0:4096].bitcast(F32)
        sinb = big2[:, 4096:8192].bitcast(F32)
        identb_s = sb("identb_s", [128, 128], BF16)
        identf_s = sb("identf_s", [128, 128], F32)
        rope_s = sb("rope_s", [128, 128], BF16)
        mask_s = sb("mask_s", [128, 128], F32)
        onesc_s = sb("onesc_s", [128, 1], BF16)
        onesr_s = sb("onesr_s", [1, 128], F32)
        esel_s = sb("esel_s", [4, 512], F32)
        sgate_s = sb("sgate_s", [128, NIC], F32)
        sup_s = sb("sup_s", [128, NIC], F32)
        sdown_s = sb("sdown_s", [128, 16], F32)
        bqkv_s = sb("bqkv_s", [128, 48], F32)
        sclv_s = sb("sclv_s", [128, 1], F32)
        bo_s = sb("bo_s", [128, 16], F32)

        pp = [psumt("pp0", [128, 512], F32), psumt("pp1", [128, 512], F32),
              psumt("pp2", [128, 512], F32)]
        ppo = psumt("ppo", [128, 512], F32)
        pr = psumt("pr", [128, 512], F32)
        ps_tt = [psumt("ps_tA", [128, 512], BF16), psumt("ps_tB", [128, 512], BF16)]
        st_a = psumt("st_a", [1, 512], F32)
        st_b = pr[0:1, 0:512]
        projb = [pp[0], pp[1], pp[2], ppo]
        projbn = ["pp0", "pp1", "pp2", "ppo"]

        SEM_NAMES = (["s_cc", "s_pe", "s_dv", "s_ac", "s_pl", "ld_cst", "ld_kfm",
                      "ld_hb", "ld_ao", "st_asm", "s_dbg"]
                     + [f"ld_x{i}" for i in range(4)]
                     + [f"ld_w{i}" for i in range(2)]
                     + [f"ld_qg{i}" for i in range(2)]
                     + [f"ld_vc{i}" for i in range(2)]
                     + [f"st_q{i}" for i in range(2)]
                     + [f"st_x{i}" for i in range(2)]
                     + [f"st_h{i}" for i in range(2)]
                     + [f"st_a{i}" for i in range(2)]
                     + ["st_wd0", "st_wd1", "ld_b0", "ld_b1", "ld_vt0", "ld_vt1", "ld_p0", "ld_p1"])
        for s in SEM_NAMES:
            SEMS[s] = stack.enter_context(nc.semaphore(s))

        def wait_streams(eng, names):
            for n in names:
                v = prog.val(n)
                prog.wait(eng, n, v)

        CTR = {"xold": 0, "xnew": 0, "xstat": 0, "qst": 0, "w32": 0, "wbf": 0,
               "yst": 0, "hst": 0}

        def get_par(e, name):
            if name not in REG:
                pid = e.partition_id()
                REG[name] = e.alloc_register(name)
                e.reg_alu(REG[name], pid, 1, ALU.bitwise_and)
            return REG[name]

        def ph_const_loads():
            for dst, src in [(identb_s, identb_in), (identf_s, identf_in),
                             (rope_s, rope_in), (mask_s, mask_in),
                             (onesc_s, onesc_in), (onesr_s, onesr_in),
                             (esel_s, esel_in), (sgate_s, s_gate), (sup_s, s_up),
                             (sdown_s, s_down), (bqkv_s, wb_bias), (bo_s, wb_bo),
                             (sclv_s, sclv_in), (cosb, cos_in), (sinb, sin_in)]:
                prog.inc("sync", "ld_cst", 16,
                         lambda e, d=dst, s=src: e.dma_start(out=d[:], in_=s))
            v = prog.val("ld_cst")
            for eng in ("vector", "scalar", "tensor", "gpsimd"):
                prog.wait(eng, "ld_cst", v)

        STATS_PEND = []
        STATS_INIT = set()

        def stats_prep(src_ap, tg):
            i = CTR["xstat"]
            CTR["xstat"] += 1
            sl = i % 4
            buf_wait("vector", ("xbf", sl))
            prog.wait("vector", "s_dv", prog.val("s_dv"))
            dv = prog.inc("vector", "s_dv", 1,
                          lambda e, o=xbf[sl], s=src_ap: e.tensor_copy(o[:], s))
            buf_wait("scalar", ("xsq", sl))
            ac = prog.inc("scalar", "s_ac", 1,
                          lambda e, o=xsq[sl], s=src_ap: e.activation(o[:], s, AF.Square))
            STATS_PEND.append((sl, tg, dv, ac))

        def stats_flush(keep=0):
            while len(STATS_PEND) > keep:
                sl, tg, dv, ac = STATS_PEND.pop(0)
                init = tg not in STATS_INIT
                STATS_INIT.add(tg)
                prog.wait("tensor", "s_dv", dv)
                prog.wait("tensor", "s_ac", ac)
                bank_wait("tensor", "st_a")
                bank_wait("tensor", "pr")
                prog.inc("tensor", "s_pe", 1,
                         lambda e, sl=sl: e.matmul(st_a[:, :], onesc_s[:, :], xbf[sl][:],
                                                   start=True, stop=True))
                pe = prog.inc("tensor", "s_pe", 1,
                              lambda e, sl=sl: e.matmul(st_b, onesc_s[:, :], xsq[sl][:],
                                                        start=True, stop=True))
                buf_set(("xbf", sl), "s_pe", pe)
                buf_set(("xsq", sl), "s_pe", pe)
                prog.wait("vector", "s_pe", pe)
                ca = scr16[0:1, tg * 512:(tg + 1) * 512]
                cb = scr16[0:1, 2048 + tg * 512:2048 + (tg + 1) * 512]
                if init:
                    prog.inc("vector", "s_dv", 1, lambda e, ca=ca: e.tensor_copy(ca, st_a[:, :]))
                    dv2 = prog.inc("vector", "s_dv", 1,
                                   lambda e, cb=cb: e.tensor_copy(cb, st_b))
                else:
                    prog.inc("vector", "s_dv", 1,
                             lambda e, ca=ca: e.tensor_tensor(ca, ca, st_a[:, :], ALU.add))
                    dv2 = prog.inc("vector", "s_dv", 1,
                                   lambda e, cb=cb: e.tensor_tensor(cb, cb, st_b, ALU.add))
                bank_set("st_a", "s_dv", dv2)
                bank_set("pr", "s_dv", dv2)

        def stats_tile(src_ap, tg, init=None):
            stats_prep(src_ap, tg)
            stats_flush(0)

        def ph_ln_full(src_r):
            STATS_INIT.clear()
            for tg in range(NTG):
                for fc in range(NFC):
                    i = CTR["xold"]
                    CTR["xold"] += 1
                    sl = i % 4
                    buf_wait("sync", ("xold", sl))
                    ld = prog.inc("sync", f"ld_x{sl}", 16,
                                  lambda e, sl=sl, fc=fc, tg=tg: e.dma_start(
                                      out=xold[sl][:], in_=src_r[:, fc, tg * 512:(tg + 1) * 512]))
                    prog.wait("vector", f"ld_x{sl}", ld)
                    prog.wait("scalar", f"ld_x{sl}", ld)
                    stats_tile(xold[sl][:], tg)
                    buf_set(("xold", sl), "s_dv", prog.val("s_dv"))
                    buf_add(("xold", sl), "s_ac", prog.val("s_ac"))

        def ph_rstd():
            prog.wait("vector", "s_dv", prog.val("s_dv"))
            m = scr16[0:1, 0:2048]
            ex2 = scr16[0:1, 2048:4096]
            t0 = big2[0:1, 8192:12288].bitcast(F32)
            dva = prog.inc("vector", "s_dv", 1,
                           lambda e: e.tensor_tensor(t0, m, m, ALU.mult))
            prog.wait("vector", "s_dv", dva)
            dvb = prog.inc("vector", "s_dv", 1,
                           lambda e: e.tensor_tensor(ex2, ex2, t0, ALU.subtract))
            prog.wait("vector", "s_dv", dvb)
            dv = prog.inc("vector", "s_dv", 1,
                          lambda e: e.tensor_scalar_add(ex2, ex2, 1e-5))
            prog.wait("scalar", "s_dv", dv)
            ac0 = prog.inc("scalar", "s_ac", 1, lambda e: e.activation(ex2, ex2, AF.Ln))
            prog.wait("vector", "s_ac", ac0)
            dv1 = prog.inc("vector", "s_dv", 1,
                           lambda e: e.tensor_scalar_mul(ex2, ex2, -0.5))
            prog.wait("scalar", "s_dv", dv1)
            ac = prog.inc("scalar", "s_ac", 1,
                          lambda e: e.activation(ex2, ex2, AF.Exp))
            prog.wait("vector", "s_ac", ac)
            dv2 = prog.inc("vector", "s_dv", 1,
                           lambda e: e.tensor_tensor(m, m, ex2, ALU.mult))
            return dv2

        def ph_apply(src_r, rdy_dv, st_barrier, act_pe_bar):
            apl_tg = [0] * NTG
            wait_streams("sync", ["st_x0", "st_x1"])
            prog.wait("vector", "s_pe", act_pe_bar)
            for tg in range(NTG):
                prog.wait("tensor", "s_dv", rdy_dv)
                bank_wait("tensor", "pp0")
                bank_wait("tensor", "pp1")
                prog.inc("tensor", "s_pe", 1,
                         lambda e, tg=tg: e.matmul(
                             pp[0][:, :], onesr_s[:, :],
                             scr16[0:1, 2048 + tg * 512:2048 + (tg + 1) * 512],
                             start=True, stop=True))
                pe = prog.inc("tensor", "s_pe", 1,
                              lambda e, tg=tg: e.matmul(
                                  pp[1][:, :], onesr_s[:, :],
                                  scr16[0:1, tg * 512:(tg + 1) * 512],
                                  start=True, stop=True))
                last_dv = 0
                for fc in range(NFC):
                    i = CTR["xold"]
                    CTR["xold"] += 1
                    sl = i % 4
                    buf_wait("sync", ("xold", sl))
                    ld = prog.inc("sync", f"ld_x{sl}", 16,
                                  lambda e, sl=sl, fc=fc, tg=tg: e.dma_start(
                                      out=xold[sl][:], in_=src_r[:, fc, tg * 512:(tg + 1) * 512]))
                    prog.wait("vector", f"ld_x{sl}", ld)
                    if fc == 0:
                        prog.wait("vector", "s_pe", pe)
                    prog.wait("vector", "s_dv", prog.val("s_dv"))
                    dvm = prog.inc("vector", "s_dv", 1,
                                   lambda e, sl=sl: e.tensor_tensor(rtmp[0][:], xold[sl][:],
                                                                    pp[0][:, :], ALU.mult))
                    prog.wait("vector", "s_dv", dvm)
                    last_dv = prog.inc("vector", "s_dv", 1,
                                       lambda e, fc=fc, tg=tg: e.tensor_tensor(
                                           act_s[:, fc, tg * 512:(tg + 1) * 512],
                                           rtmp[0][:], pp[1][:, :], ALU.subtract))
                    buf_set(("xold", sl), "s_dv", last_dv)
                bank_set("pp0", "s_dv", last_dv)
                bank_set("pp1", "s_dv", last_dv)
                apl_tg[tg] = last_dv
            return apl_tg

        def load_weight_oc(w_r, oc, nk=16):
            wi = CTR["wbf"]
            CTR["wbf"] += 1
            ws = wi % 2
            buf_wait("gpsimd", ("wbf", ws))
            pl = 0
            for hf in range((nk + 7) // 8):
                c0, c1 = hf * 8, min(nk, hf * 8 + 8)
                f = CTR["w32"]
                CTR["w32"] += 1
                fs = f % 2
                buf_wait("sync", ("w32", fs))
                ld = prog.inc("sync", f"ld_w{fs}", 16,
                              lambda e, fs=fs, c0=c0, c1=c1, oc=oc, w_r=w_r:
                              e.dma_start(out=w_f32[fs][:, 0:c1 - c0, :],
                                          in_=w_r[:, c0:c1, oc * 128:(oc + 1) * 128]))
                prog.wait("gpsimd", f"ld_w{fs}", ld)
                pl = prog.inc("gpsimd", "s_pl", 1,
                              lambda e, fs=fs, ws=ws, c0=c0, c1=c1:
                              e.tensor_copy(w_bf[ws][:, c0:c1, :], w_f32[fs][:, 0:c1 - c0, :]))
                buf_set(("w32", fs), "s_pl", pl)
            return ws, pl

        def resid_evac(oc, tg, bank_name, psum_ap, src_r, dst_r, bias_ap, scale_ap,
                       fold_stats, pe_done):
            i = CTR["xnew"]
            CTR["xnew"] += 1
            ns = i % 2
            xi = CTR["xold"]
            CTR["xold"] += 1
            sl = xi % 4
            buf_wait("sync", ("xold", sl))
            ld = prog.inc("sync", f"ld_x{sl}", 16,
                          lambda e, sl=sl, oc=oc, tg=tg: e.dma_start(
                              out=xold[sl][:], in_=src_r[:, oc, tg * 512:(tg + 1) * 512]))
            buf_wait("vector", ("xnew", ns))
            prog.wait("vector", "s_pe", pe_done)
            prog.wait("vector", f"ld_x{sl}", ld)
            if bias_ap is not None:
                dv = prog.inc("vector", "s_dv", 1,
                              lambda e, ns=ns, p=psum_ap, sl=sl, b=bias_ap:
                              e.scalar_tensor_tensor(xnew[ns][:], p, b, xold[sl][:],
                                                     ALU.add, ALU.add))
            elif scale_ap is not None:
                dv = prog.inc("vector", "s_dv", 1,
                              lambda e, ns=ns, p=psum_ap, sl=sl, s=scale_ap:
                              e.scalar_tensor_tensor(xnew[ns][:], p, s, xold[sl][:],
                                                     ALU.mult, ALU.add))
            else:
                dv = prog.inc("vector", "s_dv", 1,
                              lambda e, ns=ns, p=psum_ap, sl=sl: e.tensor_tensor(
                                  xnew[ns][:], p, xold[sl][:], ALU.add))
            bank_set(bank_name, "s_dv", dv)
            buf_set(("xold", sl), "s_dv", dv)
            prog.wait("gpsimd", "s_dv", dv)
            st = prog.inc("gpsimd", f"st_x{ns}", 16,
                          lambda e, ns=ns, oc=oc, tg=tg: e.dma_start(
                              out=dst_r[:, oc, tg * 512:(tg + 1) * 512], in_=xnew[ns][:]))
            buf_set(("xnew", ns), f"st_x{ns}", st)
            if fold_stats:
                prog.wait("scalar", "s_dv", dv)
                stats_prep(xnew[ns][:], tg)
                stats_flush(3)
                buf_add(("xnew", ns), "s_dv", prog.val("s_dv"))
                buf_add(("xnew", ns), "s_ac", prog.val("s_ac"))
            return st

        def ph_proj(w_r, noc, evac_fn, rhs_tg, oc_order=None, post_oc=None):
            order = list(oc_order if oc_order is not None else range(noc))
            ws, wrdy = load_weight_oc(w_r, order[0], 16)
            nxt = None
            for idx, oc in enumerate(order):
                prog.wait("tensor", "s_pl", wrdy)
                pes = [0] * 4
                for d in range(16):
                    for tg in range(NTG):
                        if d == 0:
                            if idx == 0:
                                prog.wait("tensor", "s_dv", rhs_tg[tg])
                            bank_wait("tensor", projbn[tg])
                        pes[tg] = prog.inc("tensor", "s_pe", 1,
                                           lambda e, d=d, tg=tg, ws=ws: e.matmul(
                                               projb[tg][:, :], w_bf[ws][:, d, :],
                                               act_s[:, d, tg * 512:(tg + 1) * 512],
                                               start=(d == 0), stop=(d == 15)))
                buf_set(("wbf", ws), "s_pe", pes[3])
                if idx + 1 < len(order):
                    ws2, wrdy2 = load_weight_oc(w_r, order[idx + 1], 16)
                else:
                    ws2, wrdy2 = None, None
                for tg in range(NTG):
                    evac_fn(oc, tg, projbn[tg], projb[tg][:, :], pes[tg])
                if post_oc is not None:
                    post_oc(idx, oc)
                ws, wrdy = ws2, wrdy2

        def make_qkv_evac(mode):
            def evac(oc, tg, bank_name, psum_ap, pe_done):
                blk = oc // 8
                typ = "q" if blk in (0, 3) else ("k" if blk in (1, 4) else "v")
                i = CTR["qst"]
                CTR["qst"] += 1
                qs = i % 2
                if mode == "b":
                    buf_wait("vector", ("qst", qs))
                else:
                    buf_wait("scalar", ("qst", qs))
                    prog.wait("scalar", "s_pe", pe_done)
                if mode == "b":
                    prog.wait("vector", "s_pe", pe_done)
                    ac = prog.inc("vector", "s_dv", 1,
                                  lambda e, qs=qs, p=psum_ap, oc=oc: e.tensor_scalar_add(
                                      qstage[qs][:], p, bqkv_s[:, oc:oc + 1]))
                    acsem = "s_dv"
                else:
                    ac = prog.inc("scalar", "s_ac", 1,
                                  lambda e, qs=qs, p=psum_ap: e.activation(
                                      qstage[qs][:], p, AF.Copy))
                    acsem = "s_ac"
                bank_set(bank_name, acsem, ac)
                dv = None
                if mode == "a" and typ in ("q", "k"):
                    prog.wait("tensor", acsem, ac)
                    bank_wait("tensor", "pr")
                    pe2 = prog.inc("tensor", "s_pe", 1,
                                   lambda e, qs=qs: e.matmul(pr[:, :], rope_s[:], qstage[qs][:],
                                                             start=True, stop=True))
                    prog.wait("vector", acsem, ac) if acsem != "s_dv" else None
                    prog.wait("vector", "s_dv", prog.val("s_dv"))
                    prog.inc("vector", "s_dv", 1,
                             lambda e, qs=qs, tg=tg: e.tensor_tensor(
                                 rtmp[1][:], qstage[qs][:],
                                 cosb[:, tg * 512:(tg + 1) * 512], ALU.mult))
                    prog.wait("vector", "s_pe", pe2)
                    dvs = prog.inc("vector", "s_dv", 1,
                                   lambda e, tg=tg: e.tensor_tensor(
                                       rtmp[0][:], pr[:, :],
                                       sinb[:, tg * 512:(tg + 1) * 512], ALU.mult))
                    prog.wait("vector", "s_dv", dvs)
                    dv = prog.inc("vector", "s_dv", 1,
                                  lambda e, qs=qs: e.tensor_tensor(
                                      qstage[qs][:], rtmp[1][:], rtmp[0][:], ALU.add))
                    bank_set("pr", "s_dv", dv)
                prog.wait("gpsimd", acsem, ac)
                if dv:
                    prog.wait("gpsimd", "s_dv", dv)
                st = prog.inc("gpsimd", f"st_q{qs}", 16,
                              lambda e, qs=qs, oc=oc, tg=tg: e.dma_start(
                                  out=qkv_loc[oc * 128:(oc + 1) * 128, tg * 512:(tg + 1) * 512],
                                  in_=qstage[qs][:]))
                buf_set(("qst", qs), f"st_q{qs}", st)
            return evac

        def ph_qkv(mode, rhs_tg, st_barrier):
            w_r = wa_qkv_r if mode == "a" else wb_qkv_r
            evac = make_qkv_evac(mode)
            marks = {}
            wait_streams("gpsimd", ["ld_kfm", "ld_qg0", "ld_qg1", "ld_vc0", "ld_vc1"])
            order = list(range(24, 48)) + list(range(24))

            def post_oc(idx, oc):
                if idx == 23:
                    wait_streams("gpsimd", ["st_q0", "st_q1"])
                    marks["cc"] = prog.inc("gpsimd", "s_cc", 1,
                                           lambda e: e.collective_compute(
                                               "AllGather", ALU.bypass,
                                               replica_groups=GROUPS,
                                               ins=[qkv_loc[3072:6144, :]],
                                               outs=[g1[:, :]]))
            ph_proj(w_r, 48, evac, rhs_tg, oc_order=order, post_oc=post_oc)
            marks["pe_end"] = prog.val("s_pe")
            return marks

        def attn_load(eng_name, sem, dst_halves, mine_row, their_row):
            v = prog.bump(sem, 32)

            def do(e):
                par = get_par(e, "par_" + eng_name)
                with e.If_eq(par, 0):
                    e.dma_start(out=dst_halves[0],
                                in_=qkv_loc[mine_row:mine_row + 128, :]).then_inc(SEMS[sem], 16)
                    e.dma_start(out=dst_halves[1],
                                in_=g1[3072 + their_row:3072 + their_row + 128, :]).then_inc(SEMS[sem], 16)
                with e.Else():
                    e.dma_start(out=dst_halves[1],
                                in_=qkv_loc[mine_row:mine_row + 128, :]).then_inc(SEMS[sem], 16)
                    e.dma_start(out=dst_halves[0],
                                in_=g1[their_row:their_row + 128, :]).then_inc(SEMS[sem], 16)
            prog.raw(eng_name, do)
            return v

        def attn_load_cols(eng_name, sem, dst, row, c0, c1):
            v = prog.bump(sem, 16)
            lo = c0 < 2048

            def do(e):
                par = get_par(e, "par_" + eng_name)
                oc0 = c0 if lo else c0 - 2048
                oc1 = c1 if lo else c1 - 2048
                with e.If_eq(par, 0):
                    srcr = (qkv_loc if lo else g1.tensor)
                    if lo:
                        e.dma_start(out=dst, in_=qkv_loc[row:row + 128, oc0:oc1]).then_inc(SEMS[sem], 16)
                    else:
                        e.dma_start(out=dst, in_=g1[3072 + row:3072 + row + 128, oc0:oc1]).then_inc(SEMS[sem], 16)
                with e.Else():
                    if lo:
                        e.dma_start(out=dst, in_=g1[row:row + 128, oc0:oc1]).then_inc(SEMS[sem], 16)
                    else:
                        e.dma_start(out=dst, in_=qkv_loc[row:row + 128, oc0:oc1]).then_inc(SEMS[sem], 16)
            prog.raw(eng_name, do)
            return v

        def ph_attention(cc1, big2_pe_bar):
            wait_streams("sync", ["st_q0", "st_q1"])
            prog.wait("sync", "s_cc", cc1)
            prog.wait("scalar", "s_pe", big2_pe_bar)
            sband = [("pp0", pp[0]), ("pp1", pp[1]), ("pp2", pp[2])]
            last_S_pe = 0
            for h in range(HM):
                buf_wait("sync", ("kfm", 0))
                kld = attn_load("sync", "ld_kfm",
                                [kfm[:, 0:2048], kfm[:, 2048:4096]],
                                1024 + h * 128, 1024 + h * 128)
                for vb in range(8):
                    i = h * 8 + vb
                    sl = i % 2
                    buf_wait("sync", ("vch", sl))
                    vld = attn_load_cols("sync", f"ld_vc{sl}", vchunk[sl][:],
                                         2048 + h * 128, vb * 512, (vb + 1) * 512)
                    prog.wait("scalar", f"ld_vc{sl}", vld)
                    buf_wait("scalar", ("vtm", 0))
                    for sub in range(4):
                        vt = prog.inc("scalar", f"ld_vt{sl}", 16,
                                      lambda e, sl=sl, sub=sub, vb=vb: e.dma_start(
                                          out=v_tm[:, (vb * 4 + sub) * 128:(vb * 4 + sub + 1) * 128],
                                          in_=vchunk[sl][:, sub * 128:(sub + 1) * 128],
                                          transpose=True))
                    buf_set(("vch", sl), f"ld_vt{sl}", prog.val(f"ld_vt{sl}"))
                prog.wait("tensor", "ld_vt0", prog.val("ld_vt0"))
                prog.wait("tensor", "ld_vt1", prog.val("ld_vt1"))
                vt_ac = prog.val("s_ac")
                for qg in range(8):
                    qsl = qg % 2
                    buf_wait("sync", ("qg", qsl))
                    qld = attn_load_cols("sync", f"ld_qg{qsl}", qg_s[qsl][:],
                                         h * 128, qg * 512, (qg + 1) * 512)
                    for qt in range(4):
                        qa = 4 * qg + qt
                        P = 128 * (qa + 1)
                        nch = (P + 511) // 512
                        nsl = qt % 2
                        if P <= 2048:
                            ri = CTR["xstat"] % 2
                            CTR["xstat"] += 1
                            base = ri * 2048
                            regions = [f"scr{ri}"]
                        else:
                            base = 0
                            regions = ["scr0", "scr1"]
                        prog.wait("tensor", "ld_kfm", kld)
                        prog.wait("tensor", f"ld_qg{qsl}", qld)
                        for r in regions:
                            buf_wait("scalar", (r, 0))
                        buf_wait("vector", ("negm", nsl))
                        for kc in range(nch):
                            N = min(512, P - kc * 512)
                            bn, pb = sband[kc % 3]
                            bank_wait("tensor", bn)
                            pe = prog.inc("tensor", "s_pe", 1,
                                          lambda e, qsl=qsl, qt=qt, kc=kc, N=N, pb=pb: e.matmul(
                                              pb[:, 0:N],
                                              qg_s[qsl][:, qt * 128:(qt + 1) * 128],
                                              kfm[:, kc * 512:kc * 512 + N],
                                              start=True, stop=True))
                            last_S_pe = pe
                            prog.wait("vector", "s_pe", pe)
                            mdv = 0
                            if kc == nch - 1:
                                mdv = prog.inc("vector", "s_dv", 1,
                                               lambda e, N=N, pb=pb: e.tensor_tensor(
                                                   pb[:, N - 128:N], pb[:, N - 128:N],
                                                   mask_s[:], ALU.add))
                                prog.wait("vector", "s_dv", mdv)
                            prog.wait("scalar", "s_pe", pe)
                            if mdv:
                                prog.wait("scalar", "s_dv", mdv)
                            eva = prog.inc("scalar", "s_ac", 1,
                                           lambda e, kc=kc, N=N, pb=pb, base=base: e.copy(
                                               scr16[:, base + kc * 512:base + kc * 512 + N],
                                               pb[:, 0:N]))
                            prog.wait("vector", "s_ac", eva)
                            ev = prog.inc("vector", "s_dv", 1,
                                          lambda e, kc=kc, N=N, nsl=nsl, base=base: e.reduce_max(
                                              nparts[nsl][:, kc:kc + 1],
                                              scr16[:, base + kc * 512:base + kc * 512 + N],
                                              axis=AX.X, negate=True))
                            bank_set(bn, "s_ac", eva)
                        prog.wait("vector", "s_dv", prog.val("s_dv"))
                        dvk = prog.inc("vector", "s_dv", 1,
                                       lambda e, nch=nch, nsl=nsl: e.tensor_reduce(
                                           negm[nsl][:], nparts[nsl][:, 0:nch],
                                           axis=AX.X, op=ALU.min))
                        prog.wait("vector", "s_dv", dvk)
                        dv = prog.inc("vector", "s_dv", 1,
                                      lambda e, nsl=nsl: e.tensor_scalar_mul(
                                          negms[nsl][:], negm[nsl][:], SCL))
                        prog.wait("scalar", "s_dv", dv)
                        prog.wait("scalar", "s_ac", prog.val("s_ac"))
                        buf_wait("scalar", ("pslot", qt))
                        if qt == 0:
                            buf_wait("scalar", ("l4", 0))
                        ac2 = prog.inc("scalar", "s_ac", 1,
                                       lambda e, qt=qt, P=P, nsl=nsl, base=base: e.activation(
                                           big2[:, qt * 4096:qt * 4096 + P],
                                           scr16[:, base:base + P],
                                           AF.Exp, bias=negms[nsl][:], scale=sclv_s[:, :],
                                           accum_out=l4[:, qt:qt + 1]))
                        buf_set(("negm", nsl), "s_ac", ac2)
                        for r in regions:
                            buf_set((r, 0), "s_ac", ac2)
                            buf_add((r, 0), "s_dv", prog.val("s_dv"))
                        prog.wait("vector", "s_ac", ac2)
                        if qt == 0:
                            buf_wait("vector", ("rl4", 0))
                        prog.inc("vector", "s_dv", 1,
                                 lambda e, qt=qt: e.reciprocal(rl4[:, qt:qt + 1],
                                                               l4[:, qt:qt + 1]))
                    buf_set(("qg", qsl), "s_pe", last_S_pe)
                    dv_rl = prog.val("s_dv")
                    exp_ac = prog.val("s_ac")
                    nj = 4 * qg + 4
                    pv_pe = 0
                    pend_pv = None

                    def emit_T(j):
                        s0 = max(0, j - 4 * qg)
                        psl = j % 2
                        prog.wait("scalar", "s_ac", exp_ac)
                        buf_wait("scalar", ("ptT", psl))
                        for qtl in range(s0, 4):
                            prog.inc("scalar", f"ld_p{psl}", 16,
                                     lambda e, qtl=qtl, j=j, psl=psl: e.dma_start(
                                         out=ptT[psl][:, qtl * 128:(qtl + 1) * 128],
                                         in_=big2[:, qtl * 4096 + j * 128:qtl * 4096 + (j + 1) * 128],
                                         transpose=True))
                        pt_ld = prog.val(f"ld_p{psl}")
                        for qtl in range(s0, 4):
                            buf_add(("pslot", qtl), f"ld_p{psl}", pt_ld)
                        return (j, s0, pt_ld)

                    def emit_PV(t):
                        j, s0, pt_ld = t
                        psl = j % 2
                        prog.wait("tensor", f"ld_p{psl}", pt_ld)
                        if j == 0:
                            bank_wait("tensor", "ppo")
                        pv = prog.inc("tensor", "s_pe", 1,
                                      lambda e, j=j, s0=s0, psl=psl, nj=nj: e.matmul(
                                          ppo[:, s0 * 128:512],
                                          v_tm[:, j * 128:(j + 1) * 128],
                                          ptT[psl][:, s0 * 128:512],
                                          start=(j == 0), stop=(j == nj - 1)))
                        buf_set(("ptT", psl), "s_pe", pv)
                        return pv

                    for j in range(nj):
                        t = emit_T(j)
                        if pend_pv is not None:
                            pv_pe = emit_PV(pend_pv)
                        pend_pv = t
                    pv_pe = emit_PV(pend_pv)
                    buf_set(("vtm", 0), "s_pe", pv_pe)
                    prog.wait("tensor", "s_dv", dv_rl)
                    bank_wait("tensor", "pr")
                    prog.inc("tensor", "s_pe", 1,
                             lambda e: e.transpose(pr[0:4, 0:128], rl4[:, :], identf_s[:]))
                    pe_t = prog.val("s_pe")
                    buf_set(("l4", 0), "s_pe", pe_t)
                    buf_set(("rl4", 0), "s_pe", pe_t)
                    prog.wait("vector", "s_pe", pe_t)
                    dv_t = prog.inc("vector", "s_dv", 1,
                                    lambda e: e.tensor_copy(rlT[:, :], pr[0:4, 0:128]))
                    prog.wait("tensor", "s_dv", dv_t)
                    for qt in range(4):
                        prog.inc("tensor", "s_pe", 1,
                                 lambda e, qt=qt: e.matmul(
                                     pr[:, qt * 128:(qt + 1) * 128],
                                     esel_s[:, qt * 128:(qt + 1) * 128], rlT[:, :],
                                     start=(qt == 0), stop=(qt == 3)))
                    rl_pe = prog.val("s_pe")
                    buf_wait("scalar", ("rlrep", 0))
                    prog.wait("scalar", "s_pe", rl_pe)
                    ac_rl = prog.inc("scalar", "s_ac", 1,
                                     lambda e: e.copy(rlrep[:, :], pr[:, :]))
                    bank_set("pr", "s_ac", ac_rl)
                    asl = qg % 2
                    buf_wait("vector", ("aost", asl))
                    prog.wait("vector", "s_pe", pv_pe)
                    prog.wait("vector", "s_ac", ac_rl)
                    ao_dv = prog.inc("vector", "s_dv", 1,
                                     lambda e, asl=asl: e.tensor_tensor(
                                         aost[asl][:], ppo[:, :], rlrep[:, :], ALU.mult))
                    bank_set("ppo", "s_dv", ao_dv)
                    buf_set(("rlrep", 0), "s_dv", ao_dv)
                    for qtl in range(4):
                        buf_set(("pslot", qtl), "s_pe", pv_pe)
                    prog.wait("gpsimd", "s_dv", ao_dv)
                    st = prog.inc("gpsimd", f"st_a{asl}", 16,
                                  lambda e, h=h, qg=qg, asl=asl: e.dma_start(
                                      out=ao_loc[h * 128:(h + 1) * 128, qg * 512:(qg + 1) * 512],
                                      in_=aost[asl][:]))
                    buf_set(("aost", asl), f"st_a{asl}", st)
                buf_set(("kfm", 0), "s_pe", last_S_pe)
                if h == 3:
                    wait_streams("gpsimd", ["st_a0", "st_a1"])
                    cc_half = prog.inc("gpsimd", "s_cc", 1,
                                       lambda e: e.collective_compute(
                                           "AllGather", ALU.bypass, replica_groups=GROUPS,
                                           ins=[ao_loc[0:512, :]], outs=[g2[0:1024, :]]))
            return {"pe": prog.val("s_pe")}

        def ph_exch2(attn_marks, qkv_pe_end):
            wait_streams("gpsimd", ["st_a0", "st_a1"])
            cc = prog.inc("gpsimd", "s_cc", 1,
                          lambda e: e.collective_compute(
                              "AllGather", ALU.bypass, replica_groups=GROUPS,
                              ins=[ao_loc[512:1024, :]], outs=[g2[1024:2048, :]]))
            prog.wait("gpsimd", "s_cc", cc)
            prog.wait("gpsimd", "s_pe", qkv_pe_end)

            def do(e):
                par = get_par(e, "par_gpsimd")
                with e.If_eq(par, 0):
                    e.dma_start(out=act_s[:, :, :],
                                in_=g2_r[:, :, 0:2048]).then_inc(SEMS["ld_ao"], 16)
                with e.Else():
                    e.dma_start(out=act_s[:, :, :],
                                in_=g2_r[:, :, 2048:4096]).then_inc(SEMS["ld_ao"], 16)
            prog.bump("ld_ao", 16)
            prog.raw("gpsimd", do)
            st = prog.val("ld_ao")
            prog.wait("tensor", "ld_ao", st)
            return st

        def ph_outproj(w_r, bias_tile, src_r, dst_r, fold_stats):
            def evac(oc, tg, bank_name, psum_ap, pe_done):
                bias_ap = bias_tile[:, oc:oc + 1] if bias_tile is not None else None
                resid_evac(oc, tg, bank_name, psum_ap, src_r, dst_r, bias_ap, None,
                           fold_stats, pe_done)
            ph_proj(w_r, 16, evac, [0, 0, 0, 0])
            return 0, prog.val("s_pe")

        def ph_gateup(rhs_tg):
            wait_streams("gpsimd", ["ld_hb"])

            def mm_block(ws, first):
                pes = [0] * 4
                for d in range(16):
                    for tg in range(NTG):
                        if d == 0:
                            if first:
                                prog.wait("tensor", "s_dv", rhs_tg[tg])
                            bank_wait("tensor", projbn[tg])
                        pes[tg] = prog.inc("tensor", "s_pe", 1,
                                           lambda e, d=d, tg=tg, ws=ws: e.matmul(
                                               projb[tg][:, :], w_bf[ws][:, d, :],
                                               act_s[:, d, tg * 512:(tg + 1) * 512],
                                               start=(d == 0), stop=(d == 15)))
                buf_set(("wbf", ws), "s_pe", pes[3])
                return pes

            ws, wrdy = load_weight_oc(w_gate_r, 0, 16)
            for ic in range(NIC):
                prog.wait("tensor", "s_pl", wrdy)
                pes = mm_block(ws, ic == 0)
                ws2, wrdy2 = load_weight_oc(w_up_r, ic, 16)
                g_acs = [0] * 4
                for tg in range(NTG):
                    prog.wait("scalar", "s_pe", pes[tg])
                    buf_wait("scalar", ("gst", tg))
                    g_acs[tg] = prog.inc("scalar", "s_ac", 1,
                                         lambda e, tg=tg, ic=ic: e.activation(
                                             gst[:, tg * 512:(tg + 1) * 512], projb[tg][:, :],
                                             AF.Silu, scale=sgate_s[:, ic:ic + 1]))
                    bank_set(projbn[tg], "s_ac", g_acs[tg])
                prog.wait("tensor", "s_pl", wrdy2)
                pes2 = mm_block(ws2, False)
                if ic + 1 < NIC:
                    ws, wrdy = load_weight_oc(w_gate_r, ic + 1, 16)
                for tg in range(NTG):
                    hi = CTR["hst"]
                    CTR["hst"] += 1
                    hsl = hi % 2
                    buf_wait("vector", ("hst", hsl))
                    prog.wait("vector", "s_pe", pes2[tg])
                    prog.wait("vector", "s_ac", g_acs[tg])
                    dv = prog.inc("vector", "s_dv", 1,
                                  lambda e, tg=tg, ic=ic, hsl=hsl: e.scalar_tensor_tensor(
                                      hst[hsl][:], projb[tg][:, :], sup_s[:, ic:ic + 1],
                                      gst[:, tg * 512:(tg + 1) * 512], ALU.mult, ALU.mult))
                    bank_set(projbn[tg], "s_dv", dv)
                    buf_set(("gst", tg), "s_dv", dv)
                    prog.wait("gpsimd", "s_dv", dv)
                    st = prog.inc("gpsimd", f"st_h{hsl}", 16,
                                  lambda e, ic=ic, tg=tg, hsl=hsl: e.dma_start(
                                      out=h_r[:, ic, tg * 512:(tg + 1) * 512], in_=hst[hsl][:]))
                    buf_set(("hst", hsl), f"st_h{hsl}", st)
            return 0, prog.val("s_pe")

        def ph_down(src_r, dst_r, fold_stats, big2_pe_bar, down_first):
            pend_holder = [None]
            WD = {}
            wait_streams("sync", ["st_h0", "st_h1"])
            prog.wait("sync", "s_pe", big2_pe_bar)
            for tg in range(NTG):
                if tg == 1 and down_first[0]:
                    wait_streams("sync", ["st_wd0", "st_wd1"])
                    down_first[0] = False
                buf_wait("sync", ("big2h", 0))
                hld = prog.inc("sync", "ld_hb", 16,
                               lambda e, tg=tg: e.dma_start(
                                   out=big2[:, 0:NIC * 512],
                                   in_=h_r[:, :, tg * 512:(tg + 1) * 512]))
                last_pe = 0

                def dsub(oc, sub, cache):
                    c0, c1 = sub * 8, min(NIC, sub * 8 + 8)
                    wi = CTR["wbf"]
                    CTR["wbf"] += 1
                    ws = wi % 2
                    if cache:
                        f = CTR["w32"]
                        CTR["w32"] += 1
                        fs = f % 2
                        buf_wait("sync", ("w32", fs))
                        ld = prog.inc("sync", f"ld_w{fs}", 16,
                                      lambda e, fs=fs, c0=c0, c1=c1, oc=oc: e.dma_start(
                                          out=w_f32[fs][:, 0:c1 - c0, :],
                                          in_=wdown_r[:, c0:c1, oc * 128:(oc + 1) * 128]))
                        buf_wait("gpsimd", ("wbf", ws))
                        prog.wait("gpsimd", f"ld_w{fs}", ld)
                        pl = prog.inc("gpsimd", "s_pl", 1,
                                      lambda e, fs=fs, ws=ws, c0=c0, c1=c1: e.tensor_copy(
                                          w_bf[ws][:, 0:c1 - c0, :], w_f32[fs][:, 0:c1 - c0, :]))
                        buf_set(("w32", fs), "s_pl", pl)
                        prog.wait("gpsimd", "s_pl", pl)
                        wd = prog.inc("gpsimd", f"st_wd{ws}", 16,
                                      lambda e, ws=ws, c0=c0, c1=c1, oc=oc: e.dma_start(
                                          out=wdownbf_r[:, c0:c1, oc * 128:(oc + 1) * 128],
                                          in_=w_bf[ws][:, 0:c1 - c0, :]))
                        WD[ws] = wd
                        return ws, "s_pl", pl, c0, c1
                    else:
                        c0, c1 = sub * 16, min(NIC, sub * 16 + 16)
                        wait_streams("sync", ["st_wd0", "st_wd1"])
                        buf_wait("sync", ("wbf", ws))
                        ld = prog.inc("sync", f"ld_b{ws}", 16,
                                      lambda e, ws=ws, c0=c0, c1=c1, oc=oc: e.dma_start(
                                          out=w_bf[ws][:, 0:c1 - c0, :],
                                          in_=wdownbf_r[:, c0:c1, oc * 128:(oc + 1) * 128]))
                        return ws, f"ld_b{ws}", ld, c0, c1

                cache = down_first[0] and tg == 0
                if tg == 0 and pend_holder[0] is None:
                    pend_holder[0] = dsub(0, 0, cache)
                nsub = 6 if cache else 3
                for oc in range(16):
                    bank = projbn[oc % 2]
                    pb = projb[oc % 2]
                    pe_done = 0
                    first = True
                    for sub in range(nsub):
                        ws, plsem, pl, c0, c1 = pend_holder[0]
                        prog.wait("tensor", plsem, pl)
                        if first:
                            prog.wait("tensor", "ld_hb", hld)
                            bank_wait("tensor", bank)
                        if sub + 1 < nsub:
                            pend_holder[0] = dsub(oc, sub + 1, cache)
                        elif oc + 1 < 16:
                            pend_holder[0] = dsub(oc + 1, 0, cache)
                        elif tg + 1 < NTG:
                            pend_holder[0] = dsub(0, 0, False)
                        else:
                            pend_holder[0] = None
                        for ici, ic in enumerate(range(c0, c1)):
                            pe_done = prog.inc("tensor", "s_pe", 1,
                                               lambda e, ws=ws, ici=ici, ic=ic, pb=pb,
                                               f=(first and ici == 0): e.matmul(
                                                   pb[:, :], w_bf[ws][:, ici, :],
                                                   big2[:, ic * 512:(ic + 1) * 512],
                                                   start=f, stop=(ic == NIC - 1)))
                        first = False
                        buf_set(("wbf", ws), "s_pe", pe_done)
                        if WD.get(ws) is not None:
                            buf_add(("wbf", ws), f"st_wd{ws}", WD.pop(ws))
                    last_pe = pe_done
                    resid_evac(oc, tg, bank, pb[:, :], src_r, dst_r, None,
                               sdown_s[:, oc:oc + 1], fold_stats, pe_done)
                buf_set(("big2h", 0), "s_pe", last_pe)
            return 0, prog.val("s_pe")

        down_first = [True]
        ph_const_loads()
        ph_ln_full(xin_r)

        st_bar = 0
        act_pe_bar = 0
        big2_pe_bar = 0
        attn_pe_end = 0

        sub_srcs = [xin_r, xstr_r, xstr_r, xstr_r]
        for sub in range(n_sub):
            dst_r = yout_r if sub == n_sub - 1 else xstr_r
            fold = sub < n_sub - 1
            src_r = sub_srcs[sub]
            stats_flush(0)
            rdy = ph_rstd()
            apl_tg = ph_apply(src_r, rdy, st_bar, act_pe_bar)
            if sub % 2 == 0:
                mode = "a" if sub == 0 else "b"
                marks = ph_qkv(mode, apl_tg, st_bar)
                if stop_after == "asm":
                    break
                am = ph_attention(marks["cc"], big2_pe_bar)
                attn_pe_end = am["pe"]
                big2_pe_bar = am["pe"]
                ph_exch2(am, marks["pe_end"])
                w_r = wa_proj_r if mode == "a" else wb_o_r
                btile = None if mode == "a" else bo_s
                STATS_INIT.clear()
                st_bar, act_pe_bar = ph_outproj(w_r, btile, src_r, dst_r, fold)
            else:
                gum_st, gum_pe = ph_gateup(apl_tg)
                act_pe_bar = gum_pe
                STATS_INIT.clear()
                st_bar, big2_pe_bar = ph_down(src_r, dst_r, fold, big2_pe_bar, down_first)

        ST_ALL = ["st_q0", "st_q1", "st_x0", "st_x1", "st_h0", "st_h1",
                  "st_a0", "st_a1", "st_asm"]
        if dbg:
            wait_streams("gpsimd", ST_ALL)
            prog.inc("gpsimd", "s_dbg", 16,
                     lambda e: e.dma_start(out=dbg_qkv, in_=qkvabs[:, :]))
            if stop_after is None:
                prog.inc("gpsimd", "s_dbg", 16,
                         lambda e: e.dma_start(out=dbg_ao, in_=g2[:, :]))
            if stop_after == "asm":
                prog.inc("gpsimd", "s_dbg", 16,
                         lambda e: e.dma_start(out=dbg_loc, in_=qkv_loc[:, :]))
                prog.inc("gpsimd", "s_dbg", 16,
                         lambda e: e.dma_start(out=dbg_xln_r, in_=act_s[:, :, :]))
            prog.wait("gpsimd", "s_dbg", prog.val("s_dbg"))
        wait_streams("gpsimd", ST_ALL)

        def replay(name):
            def f(e):
                for item in prog.ops[name]:
                    k = item[0]
                    if k == "wait":
                        e.wait_ge(SEMS[item[1]], item[2])
                    elif k == "inc":
                        item[3](e).then_inc(SEMS[item[1]], item[2])
                    else:
                        item[1](e)
            return f

        with nc.Block() as block:
            block.sync(replay("sync"))
            block.vector(replay("vector"))
            block.scalar(replay("scalar"))
            block.tensor(replay("tensor"))
            block.gpsimd(replay("gpsimd"))

    return nc


def _stage_core(inputs, core):
    p, e = core >> 1, core & 1
    f32 = np.float32
    x = np.asarray(inputs["L0_attn_a_x"], f32)
    xin = np.ascontiguousarray(x[p, e * TK:(e + 1) * TK, :].T)

    mine = list(range(8)) if e == 0 else list(range(8, 16))
    theirs = list(range(8, 16)) if e == 0 else list(range(8))

    def perm_cols(wT):
        qs = [wT[:, h * 128:(h + 1) * 128] for h in range(16)]
        ks = [wT[:, 2048 + h * 128:2048 + (h + 1) * 128] for h in range(16)]
        vs = [wT[:, 4096 + h * 128:4096 + (h + 1) * 128] for h in range(16)]
        blocks = ([qs[h] for h in mine] + [ks[h] for h in mine] + [vs[h] for h in mine]
                  + [qs[h] for h in theirs] + [ks[h] for h in theirs]
                  + [vs[h] for h in theirs])
        return np.ascontiguousarray(np.concatenate(blocks, axis=1))

    wa_qkv = perm_cols(np.asarray(inputs["L0_attn_a_w_qkv"], f32).T)
    gperm = [0, 1, 2, 3, 8, 9, 10, 11, 4, 5, 6, 7, 12, 13, 14, 15]

    def perm_rows(w):
        return np.ascontiguousarray(
            np.concatenate([w[h * 128:(h + 1) * 128, :] for h in gperm], axis=0))

    wa_proj = perm_rows(np.asarray(inputs["L0_attn_a_w_proj"], f32).T)

    Wq = np.asarray(inputs["L0_attn_b_Wq"], f32)
    Wk = np.asarray(inputs["L0_attn_b_Wk"], f32)
    Wv = np.asarray(inputs["L0_attn_b_Wv"], f32)
    wb_cat = np.concatenate([np.concatenate([Wq[h] for h in range(16)], axis=1),
                             np.concatenate([Wk[h] for h in range(16)], axis=1),
                             np.concatenate([Wv[h] for h in range(16)], axis=1)], axis=1)
    wb_qkv = perm_cols(wb_cat)
    bq = np.asarray(inputs["L0_attn_b_bq"], f32)
    bk = np.asarray(inputs["L0_attn_b_bk"], f32)
    bv = np.asarray(inputs["L0_attn_b_bv"], f32)
    b6 = np.concatenate([bq[h] for h in mine] + [bk[h] for h in mine]
                        + [bv[h] for h in mine] + [bq[h] for h in theirs]
                        + [bk[h] for h in theirs] + [bv[h] for h in theirs])
    wb_bias = np.ascontiguousarray(b6.reshape(48, 128).T)
    wb_o = perm_rows(np.asarray(inputs["L0_attn_b_Wo"], f32))
    wb_bo = np.ascontiguousarray(np.asarray(inputs["L0_attn_b_bo"], f32).reshape(16, 128).T)

    w_gate = np.ascontiguousarray(np.asarray(inputs["L0_ffn_gate_w"], f32).T)
    w_up = np.ascontiguousarray(np.asarray(inputs["L0_ffn_up_w"], f32).T)
    w_down = np.ascontiguousarray(np.asarray(inputs["L0_ffn_down_w"], f32).T)
    s_gate = np.ascontiguousarray(np.asarray(inputs["L0_ffn_gate_s"], f32).reshape(NIC, 128).T)
    s_up = np.ascontiguousarray(np.asarray(inputs["L0_ffn_up_s"], f32).reshape(NIC, 128).T)
    s_down = np.ascontiguousarray(np.asarray(inputs["L0_ffn_down_s"], f32).reshape(16, 128).T)

    pos = np.arange(e * TK, (e + 1) * TK, dtype=np.float64)
    inv = 10000.0 ** (-np.arange(0, 128, 2, dtype=np.float64) / 128.0)
    freqs = np.outer(pos, inv)
    emb = np.concatenate([freqs, freqs], axis=1)
    cos_in = np.ascontiguousarray(np.cos(emb).T.astype(f32))
    sin_in = np.ascontiguousarray(np.sin(emb).T.astype(f32))

    M = np.zeros((128, 128), f32)
    for d in range(64):
        M[d + 64, d] = -1.0
    for d in range(64, 128):
        M[d - 64, d] = 1.0

    esel = np.zeros((4, 512), f32)
    for qt in range(4):
        esel[qt, qt * 128:(qt + 1) * 128] = 1.0

    return {
        "xin": xin, "wa_qkv": wa_qkv, "wa_proj": wa_proj, "wb_qkv": wb_qkv,
        "wb_bias": wb_bias, "wb_o": wb_o, "wb_bo": wb_bo, "w_gate": w_gate,
        "w_up": w_up, "w_down": w_down, "s_gate": s_gate, "s_up": s_up,
        "s_down": s_down, "cos_in": cos_in, "sin_in": sin_in,
        "rope_in": M.astype(ml_dtypes.bfloat16),
        "identb_in": np.eye(128, dtype=ml_dtypes.bfloat16),
        "identf_in": np.eye(128, dtype=np.float32),
        "mask_in": np.triu(np.full((128, 128), NEG, f32), k=1),
        "onesc_in": np.full((128, 1), 1.0 / 2048.0, ml_dtypes.bfloat16),
        "onesr_in": np.ones((1, 128), f32),
        "esel_in": esel,
        "sclv_in": np.full((128, 1), SCL, f32),
    }


def kernel(**inputs):
    nc = build_program(N_SUB)
    in_maps = [_stage_core(inputs, c) for c in range(8)]
    res = run_bass_kernel_spmd(nc, in_maps, core_ids=list(range(8)))
    out = np.empty((4, T, D), np.float32)
    for c in range(8):
        p, e = c >> 1, c & 1
        out[p, e * TK:(e + 1) * TK, :] = res.results[c]["y_out"].T
    return out


# version 08 (final)
# speedup vs baseline: 1.5926x; 1.5926x over previous
import math
import contextlib
import numpy as np
import ml_dtypes
import concourse.bass as bass
import concourse.mybir as mybir
from concourse.bass_utils import run_bass_kernel_spmd

F32 = mybir.dt.float32
BF16 = mybir.dt.bfloat16
AF = mybir.ActivationFunctionType
ALU = mybir.AluOpType
AX = mybir.AxisListType

D = 2048
TK = 2048
T = 4096
NFC = 16
NTG = 4
HM = 8
INTER = 5632
NIC = 44
SCL = 1.0 / math.sqrt(128.0)
NEG = -1.0e30
N_SUB = 4

ENGINES = ("sync", "vector", "scalar", "tensor", "gpsimd")
GROUPS = [[0, 1], [2, 3], [4, 5], [6, 7]]


class Prog:
    def __init__(self):
        self.ops = {e: [] for e in ENGINES}
        self.cnt = {}

    def val(self, s):
        return self.cnt.get(s, 0)

    def bump(self, s, amt):
        self.cnt[s] = self.cnt.get(s, 0) + amt
        return self.cnt[s]

    def inc(self, eng, s, amt, fn):
        v = self.bump(s, amt)
        self.ops[eng].append(("inc", s, amt, fn))
        return v

    def wait(self, eng, s, v):
        if v > 0:
            self.ops[eng].append(("wait", s, v))

    def raw(self, eng, fn):
        self.ops[eng].append(("raw", fn))


def build_program(n_sub=N_SUB, dbg=False, stop_after=None):
    nc = bass.Bass("TRN2", target_bir_lowering=False, debug=False)

    def din(name, shape, dt=F32):
        return nc.dram_tensor(name, shape, dt, kind="ExternalInput").ap()

    xin = din("xin", [D, TK])
    wa_qkv = din("wa_qkv", [D, 3 * D])
    wa_proj = din("wa_proj", [D, D])
    wb_qkv = din("wb_qkv", [D, 3 * D])
    wb_bias = din("wb_bias", [128, 48])
    wb_o = din("wb_o", [D, D])
    wb_bo = din("wb_bo", [128, 16])
    w_gate = din("w_gate", [D, INTER])
    w_up = din("w_up", [D, INTER])
    w_down = din("w_down", [INTER, D])
    s_gate = din("s_gate", [128, NIC])
    s_up = din("s_up", [128, NIC])
    s_down = din("s_down", [128, 16])
    cos_in = din("cos_in", [128, TK])
    sin_in = din("sin_in", [128, TK])
    rope_in = din("rope_in", [128, 128], BF16)
    identb_in = din("identb_in", [128, 128], BF16)
    identf_in = din("identf_in", [128, 128], F32)
    mask_in = din("mask_in", [128, 128])
    onesc_in = din("onesc_in", [128, 1], BF16)
    onesr_in = din("onesr_in", [1, 128])
    esel_in = din("esel_in", [4, 512])
    sclv_in = din("sclv_in", [128, 1])

    y_out = nc.dram_tensor("y_out", [D, TK], F32, kind="ExternalOutput").ap()
    if dbg:
        dbg_qkv = nc.dram_tensor("dbg_qkv", [3072, T], BF16, kind="ExternalOutput").ap()
        dbg_ao = nc.dram_tensor("dbg_ao", [2048, T], BF16, kind="ExternalOutput").ap()
        dbg_loc = nc.dram_tensor("dbg_loc", [6144, TK], BF16, kind="ExternalOutput").ap()
        dbg_xln = nc.dram_tensor("dbg_xln", [D, TK], BF16, kind="ExternalOutput").ap()
        dbg_xln_r = dbg_xln.rearrange("(c p) t -> p c t", p=128)

    x_str = nc.dram_tensor("x_str", [D, TK], F32).ap()
    qkv_loc = nc.dram_tensor("qkv_loc", [6144, TK], BF16).ap()
    g1 = nc.dram_tensor("g1", [2 * 3072, TK], BF16).ap()
    qkvabs = nc.dram_tensor("qkvabs", [3072, T], BF16).ap()
    ao_loc = nc.dram_tensor("ao_loc", [1024, T], BF16).ap()
    g2 = nc.dram_tensor("g2", [2048, T], BF16).ap()
    h_dram = nc.dram_tensor("h_dram", [INTER, TK], BF16).ap()
    down_bfd = nc.dram_tensor("down_bfd", [INTER, D], BF16).ap()

    xin_r = xin.rearrange("(c p) t -> p c t", p=128)
    xstr_r = x_str.rearrange("(c p) t -> p c t", p=128)
    yout_r = y_out.rearrange("(c p) t -> p c t", p=128)
    g2_r = g2.rearrange("(c p) t -> p c t", p=128)
    h_r = h_dram.rearrange("(c p) t -> p c t", p=128)

    def wview(w):
        return w.rearrange("(c p) m -> p c m", p=128)

    wa_qkv_r, wb_qkv_r = wview(wa_qkv), wview(wb_qkv)
    wa_proj_r, wb_o_r = wview(wa_proj), wview(wb_o)
    w_gate_r, w_up_r = wview(w_gate), wview(w_up)
    wdown_r = wview(w_down)
    wdownbf_r = down_bfd.rearrange("(c p) m -> p c m", p=128)

    prog = Prog()
    SEMS = {}
    REG = {}
    BANKF = {}
    BUFF = {}

    def bank_wait(eng, name):
        for s, v in BANKF.get(name, {}).items():
            prog.wait(eng, s, v)

    def bank_set(name, sem, val):
        BANKF[name] = {sem: val}

    def buf_wait(eng, key):
        for s, v in BUFF.get(key, {}).items():
            prog.wait(eng, s, v)

    def buf_set(key, sem, val):
        BUFF[key] = {sem: val}

    def buf_add(key, sem, val):
        BUFF.setdefault(key, {})[sem] = val

    stack = contextlib.ExitStack()

    def sb(name, shape, dt):
        return stack.enter_context(nc.sbuf_tensor(name, shape, dt))

    def psumt(name, shape, dt):
        return stack.enter_context(nc.psum_tensor(name, shape, dt))

    with stack:
        act_s = sb("act_s", [128, 16, 2048], BF16)
        big2 = sb("big2", [128, 22528], BF16)
        scr16 = sb("scr16", [128, 4096], F32)
        kfm = sb("kfm", [128, T], BF16)
        qg_t = sb("qg_s", [128, 2, 512], BF16)
        qg_s = [qg_t[:, i, :] for i in range(2)]
        vch_t = sb("vchunk", [128, 2, 512], BF16)
        vchunk = [vch_t[:, i, :] for i in range(2)]
        v_tm = sb("v_tm", [128, T], BF16)
        w32_t = sb("w_f32", [128, 2, 8, 128], F32)
        w_f32 = [w32_t[:, i, :, :] for i in range(2)]
        wbf_t = sb("w_bf", [128, 2, 16, 128], BF16)
        w_bf = [wbf_t[:, i, :, :] for i in range(2)]
        xold_t = sb("xold", [128, 4, 512], F32)
        xold = [xold_t[:, i, :] for i in range(4)]
        xnew_t = sb("xnew", [128, 2, 512], F32)
        xnew = [xnew_t[:, i, :] for i in range(2)]
        xbf_t = sb("xbf", [128, 4, 512], BF16)
        xbf = [xbf_t[:, i, :] for i in range(4)]
        xsq_t = sb("xsq", [128, 4, 512], BF16)
        xsq = [xsq_t[:, i, :] for i in range(4)]
        qst_t = sb("qstage", [128, 2, 512], BF16)
        qstage = [qst_t[:, i, :] for i in range(2)]
        rtmp_t = sb("rtmp", [128, 2, 512], F32)
        rtmp = [rtmp_t[:, i, :] for i in range(2)]
        ptT_t = sb("ptT", [128, 2, 512], BF16)
        ptT = [ptT_t[:, i, :] for i in range(2)]
        aost_t = sb("aost", [128, 2, 512], BF16)
        aost = [aost_t[:, i, :] for i in range(2)]
        rlrep = sb("rlrep", [128, 512], F32)
        gst = sb("gst", [128, 2048], BF16)
        hst_t = sb("hst", [128, 2, 512], BF16)
        hst = [hst_t[:, i, :] for i in range(2)]
        nparts_t = sb("nparts", [128, 16], F32)
        nparts = [nparts_t[:, 0:8], nparts_t[:, 8:16]]
        l4 = sb("l4", [128, 4], F32)
        rl4 = sb("rl4", [128, 4], F32)
        rlT = sb("rlT", [4, 128], F32)
        negm_t = sb("negm", [128, 2], F32)
        negm = [negm_t[:, i:i + 1] for i in range(2)]
        negms_t = sb("negms", [128, 2], F32)
        negms = [negms_t[:, i:i + 1] for i in range(2)]
        cosb = big2[:, 0:4096].bitcast(F32)
        sinb = big2[:, 4096:8192].bitcast(F32)
        identb_s = sb("identb_s", [128, 128], BF16)
        identf_s = sb("identf_s", [128, 128], F32)
        rope_s = sb("rope_s", [128, 128], BF16)
        mask_s = sb("mask_s", [128, 128], F32)
        onesc_s = sb("onesc_s", [128, 1], BF16)
        onesr_s = sb("onesr_s", [1, 128], F32)
        esel_s = sb("esel_s", [4, 512], F32)
        sgate_s = sb("sgate_s", [128, NIC], F32)
        sup_s = sb("sup_s", [128, NIC], F32)
        sdown_s = sb("sdown_s", [128, 16], F32)
        bqkv_s = sb("bqkv_s", [128, 48], F32)
        sclv_s = sb("sclv_s", [128, 1], F32)
        bo_s = sb("bo_s", [128, 16], F32)

        pp = [psumt("pp0", [128, 512], F32), psumt("pp1", [128, 512], F32),
              psumt("pp2", [128, 512], F32)]
        ppo = psumt("ppo", [128, 512], F32)
        pr = psumt("pr", [128, 512], F32)
        ps_tt = [psumt("ps_tA", [128, 512], BF16), psumt("ps_tB", [128, 512], BF16)]
        st_a = psumt("st_a", [1, 512], F32)
        st_b = pr[0:1, 0:512]
        projb = [pp[0], pp[1], pp[2], ppo]
        projbn = ["pp0", "pp1", "pp2", "ppo"]

        SEM_NAMES = (["s_cc", "s_pe", "s_dv", "s_ac", "s_pl", "ld_cst", "ld_kfm",
                      "ld_hb", "ld_ao", "st_asm", "s_dbg"]
                     + [f"ld_x{i}" for i in range(4)]
                     + [f"ld_w{i}" for i in range(2)]
                     + [f"ld_qg{i}" for i in range(2)]
                     + [f"ld_vc{i}" for i in range(2)]
                     + [f"st_q{i}" for i in range(2)]
                     + [f"st_x{i}" for i in range(2)]
                     + [f"st_h{i}" for i in range(2)]
                     + [f"st_a{i}" for i in range(2)]
                     + ["st_wd0", "st_wd1", "ld_b0", "ld_b1", "ld_vt0", "ld_vt1", "ld_p0", "ld_p1"])
        for s in SEM_NAMES:
            SEMS[s] = stack.enter_context(nc.semaphore(s))

        def wait_streams(eng, names):
            for n in names:
                v = prog.val(n)
                prog.wait(eng, n, v)

        CTR = {"xold": 0, "xnew": 0, "xstat": 0, "qst": 0, "w32": 0, "wbf": 0,
               "yst": 0, "hst": 0}

        def get_par(e, name):
            if name not in REG:
                pid = e.partition_id()
                REG[name] = e.alloc_register(name)
                e.reg_alu(REG[name], pid, 1, ALU.bitwise_and)
            return REG[name]

        def ph_const_loads():
            for dst, src in [(identb_s, identb_in), (identf_s, identf_in),
                             (rope_s, rope_in), (mask_s, mask_in),
                             (onesc_s, onesc_in), (onesr_s, onesr_in),
                             (esel_s, esel_in), (sgate_s, s_gate), (sup_s, s_up),
                             (sdown_s, s_down), (bqkv_s, wb_bias), (bo_s, wb_bo),
                             (sclv_s, sclv_in), (cosb, cos_in), (sinb, sin_in)]:
                prog.inc("sync", "ld_cst", 16,
                         lambda e, d=dst, s=src: e.dma_start(out=d[:], in_=s))
            v = prog.val("ld_cst")
            for eng in ("vector", "scalar", "tensor", "gpsimd"):
                prog.wait(eng, "ld_cst", v)

        STATS_PEND = []
        STATS_INIT = set()

        def stats_prep(src_ap, tg):
            i = CTR["xstat"]
            CTR["xstat"] += 1
            sl = i % 4
            buf_wait("vector", ("xbf", sl))
            prog.wait("vector", "s_dv", prog.val("s_dv"))
            dv = prog.inc("vector", "s_dv", 1,
                          lambda e, o=xbf[sl], s=src_ap: e.tensor_copy(o[:], s))
            buf_wait("scalar", ("xsq", sl))
            ac = prog.inc("scalar", "s_ac", 1,
                          lambda e, o=xsq[sl], s=src_ap: e.activation(o[:], s, AF.Square))
            STATS_PEND.append((sl, tg, dv, ac))

        def stats_flush(keep=0):
            while len(STATS_PEND) > keep:
                sl, tg, dv, ac = STATS_PEND.pop(0)
                init = tg not in STATS_INIT
                STATS_INIT.add(tg)
                prog.wait("tensor", "s_dv", dv)
                prog.wait("tensor", "s_ac", ac)
                bank_wait("tensor", "st_a")
                bank_wait("tensor", "pr")
                prog.inc("tensor", "s_pe", 1,
                         lambda e, sl=sl: e.matmul(st_a[:, :], onesc_s[:, :], xbf[sl][:],
                                                   start=True, stop=True))
                pe = prog.inc("tensor", "s_pe", 1,
                              lambda e, sl=sl: e.matmul(st_b, onesc_s[:, :], xsq[sl][:],
                                                        start=True, stop=True))
                buf_set(("xbf", sl), "s_pe", pe)
                buf_set(("xsq", sl), "s_pe", pe)
                prog.wait("vector", "s_pe", pe)
                ca = scr16[0:1, tg * 512:(tg + 1) * 512]
                cb = scr16[0:1, 2048 + tg * 512:2048 + (tg + 1) * 512]
                if init:
                    prog.inc("vector", "s_dv", 1, lambda e, ca=ca: e.tensor_copy(ca, st_a[:, :]))
                    dv2 = prog.inc("vector", "s_dv", 1,
                                   lambda e, cb=cb: e.tensor_copy(cb, st_b))
                else:
                    prog.inc("vector", "s_dv", 1,
                             lambda e, ca=ca: e.tensor_tensor(ca, ca, st_a[:, :], ALU.add))
                    dv2 = prog.inc("vector", "s_dv", 1,
                                   lambda e, cb=cb: e.tensor_tensor(cb, cb, st_b, ALU.add))
                bank_set("st_a", "s_dv", dv2)
                bank_set("pr", "s_dv", dv2)

        def stats_tile(src_ap, tg, init=None):
            stats_prep(src_ap, tg)
            stats_flush(0)

        def ph_ln_full(src_r):
            STATS_INIT.clear()
            for tg in range(NTG):
                for fc in range(NFC):
                    i = CTR["xold"]
                    CTR["xold"] += 1
                    sl = i % 4
                    buf_wait("sync", ("xold", sl))
                    ld = prog.inc("sync", f"ld_x{sl}", 16,
                                  lambda e, sl=sl, fc=fc, tg=tg: e.dma_start(
                                      out=xold[sl][:], in_=src_r[:, fc, tg * 512:(tg + 1) * 512]))
                    prog.wait("vector", f"ld_x{sl}", ld)
                    prog.wait("scalar", f"ld_x{sl}", ld)
                    stats_tile(xold[sl][:], tg)
                    buf_set(("xold", sl), "s_dv", prog.val("s_dv"))
                    buf_add(("xold", sl), "s_ac", prog.val("s_ac"))

        def ph_rstd():
            prog.wait("vector", "s_dv", prog.val("s_dv"))
            m = scr16[0:1, 0:2048]
            ex2 = scr16[0:1, 2048:4096]
            t0 = big2[0:1, 8192:12288].bitcast(F32)
            dva = prog.inc("vector", "s_dv", 1,
                           lambda e: e.tensor_tensor(t0, m, m, ALU.mult))
            prog.wait("vector", "s_dv", dva)
            dvb = prog.inc("vector", "s_dv", 1,
                           lambda e: e.tensor_tensor(ex2, ex2, t0, ALU.subtract))
            prog.wait("vector", "s_dv", dvb)
            dv = prog.inc("vector", "s_dv", 1,
                          lambda e: e.tensor_scalar_add(ex2, ex2, 1e-5))
            prog.wait("scalar", "s_dv", dv)
            ac0 = prog.inc("scalar", "s_ac", 1, lambda e: e.activation(ex2, ex2, AF.Ln))
            prog.wait("vector", "s_ac", ac0)
            dv1 = prog.inc("vector", "s_dv", 1,
                           lambda e: e.tensor_scalar_mul(ex2, ex2, -0.5))
            prog.wait("scalar", "s_dv", dv1)
            ac = prog.inc("scalar", "s_ac", 1,
                          lambda e: e.activation(ex2, ex2, AF.Exp))
            prog.wait("vector", "s_ac", ac)
            dv2 = prog.inc("vector", "s_dv", 1,
                           lambda e: e.tensor_tensor(m, m, ex2, ALU.mult))
            return dv2

        def ph_apply(src_r, rdy_dv, st_barrier, act_pe_bar):
            apl_tg = [0] * NTG
            wait_streams("sync", ["st_x0", "st_x1"])
            prog.wait("vector", "s_pe", act_pe_bar)
            for tg in range(NTG):
                prog.wait("tensor", "s_dv", rdy_dv)
                bank_wait("tensor", "pp0")
                bank_wait("tensor", "pp1")
                prog.inc("tensor", "s_pe", 1,
                         lambda e, tg=tg: e.matmul(
                             pp[0][:, :], onesr_s[:, :],
                             scr16[0:1, 2048 + tg * 512:2048 + (tg + 1) * 512],
                             start=True, stop=True))
                pe = prog.inc("tensor", "s_pe", 1,
                              lambda e, tg=tg: e.matmul(
                                  pp[1][:, :], onesr_s[:, :],
                                  scr16[0:1, tg * 512:(tg + 1) * 512],
                                  start=True, stop=True))
                last_dv = 0
                for fc in range(NFC):
                    i = CTR["xold"]
                    CTR["xold"] += 1
                    sl = i % 4
                    buf_wait("sync", ("xold", sl))
                    ld = prog.inc("sync", f"ld_x{sl}", 16,
                                  lambda e, sl=sl, fc=fc, tg=tg: e.dma_start(
                                      out=xold[sl][:], in_=src_r[:, fc, tg * 512:(tg + 1) * 512]))
                    prog.wait("vector", f"ld_x{sl}", ld)
                    if fc == 0:
                        prog.wait("vector", "s_pe", pe)
                    prog.wait("vector", "s_dv", prog.val("s_dv"))
                    dvm = prog.inc("vector", "s_dv", 1,
                                   lambda e, sl=sl: e.tensor_tensor(rtmp[0][:], xold[sl][:],
                                                                    pp[0][:, :], ALU.mult))
                    prog.wait("vector", "s_dv", dvm)
                    last_dv = prog.inc("vector", "s_dv", 1,
                                       lambda e, fc=fc, tg=tg: e.tensor_tensor(
                                           act_s[:, fc, tg * 512:(tg + 1) * 512],
                                           rtmp[0][:], pp[1][:, :], ALU.subtract))
                    buf_set(("xold", sl), "s_dv", last_dv)
                bank_set("pp0", "s_dv", last_dv)
                bank_set("pp1", "s_dv", last_dv)
                apl_tg[tg] = last_dv
            return apl_tg

        def load_weight_oc(w_r, oc, nk=16):
            wi = CTR["wbf"]
            CTR["wbf"] += 1
            ws = wi % 2
            buf_wait("gpsimd", ("wbf", ws))
            pl = 0
            for hf in range((nk + 7) // 8):
                c0, c1 = hf * 8, min(nk, hf * 8 + 8)
                f = CTR["w32"]
                CTR["w32"] += 1
                fs = f % 2
                buf_wait("sync", ("w32", fs))
                ld = prog.inc("sync", f"ld_w{fs}", 16,
                              lambda e, fs=fs, c0=c0, c1=c1, oc=oc, w_r=w_r:
                              e.dma_start(out=w_f32[fs][:, 0:c1 - c0, :],
                                          in_=w_r[:, c0:c1, oc * 128:(oc + 1) * 128]))
                prog.wait("gpsimd", f"ld_w{fs}", ld)
                pl = prog.inc("gpsimd", "s_pl", 1,
                              lambda e, fs=fs, ws=ws, c0=c0, c1=c1:
                              e.tensor_copy(w_bf[ws][:, c0:c1, :], w_f32[fs][:, 0:c1 - c0, :]))
                buf_set(("w32", fs), "s_pl", pl)
            return ws, pl

        def resid_evac(oc, tg, bank_name, psum_ap, src_r, dst_r, bias_ap, scale_ap,
                       fold_stats, pe_done):
            i = CTR["xnew"]
            CTR["xnew"] += 1
            ns = i % 2
            xi = CTR["xold"]
            CTR["xold"] += 1
            sl = xi % 4
            buf_wait("sync", ("xold", sl))
            ld = prog.inc("sync", f"ld_x{sl}", 16,
                          lambda e, sl=sl, oc=oc, tg=tg: e.dma_start(
                              out=xold[sl][:], in_=src_r[:, oc, tg * 512:(tg + 1) * 512]))
            buf_wait("vector", ("xnew", ns))
            prog.wait("vector", "s_pe", pe_done)
            prog.wait("vector", f"ld_x{sl}", ld)
            if bias_ap is not None:
                dv = prog.inc("vector", "s_dv", 1,
                              lambda e, ns=ns, p=psum_ap, sl=sl, b=bias_ap:
                              e.scalar_tensor_tensor(xnew[ns][:], p, b, xold[sl][:],
                                                     ALU.add, ALU.add))
            elif scale_ap is not None:
                dv = prog.inc("vector", "s_dv", 1,
                              lambda e, ns=ns, p=psum_ap, sl=sl, s=scale_ap:
                              e.scalar_tensor_tensor(xnew[ns][:], p, s, xold[sl][:],
                                                     ALU.mult, ALU.add))
            else:
                dv = prog.inc("vector", "s_dv", 1,
                              lambda e, ns=ns, p=psum_ap, sl=sl: e.tensor_tensor(
                                  xnew[ns][:], p, xold[sl][:], ALU.add))
            bank_set(bank_name, "s_dv", dv)
            buf_set(("xold", sl), "s_dv", dv)
            prog.wait("gpsimd", "s_dv", dv)
            st = prog.inc("gpsimd", f"st_x{ns}", 16,
                          lambda e, ns=ns, oc=oc, tg=tg: e.dma_start(
                              out=dst_r[:, oc, tg * 512:(tg + 1) * 512], in_=xnew[ns][:]))
            buf_set(("xnew", ns), f"st_x{ns}", st)
            if fold_stats:
                prog.wait("scalar", "s_dv", dv)
                stats_prep(xnew[ns][:], tg)
                stats_flush(3)
                buf_add(("xnew", ns), "s_dv", prog.val("s_dv"))
                buf_add(("xnew", ns), "s_ac", prog.val("s_ac"))
            return st

        def ph_proj(w_r, noc, evac_fn, rhs_tg, oc_order=None, post_oc=None):
            order = list(oc_order if oc_order is not None else range(noc))
            ws, wrdy = load_weight_oc(w_r, order[0], 16)
            nxt = None
            for idx, oc in enumerate(order):
                prog.wait("tensor", "s_pl", wrdy)
                pes = [0] * 4
                for d in range(16):
                    for tg in range(NTG):
                        if d == 0:
                            if idx == 0:
                                prog.wait("tensor", "s_dv", rhs_tg[tg])
                            bank_wait("tensor", projbn[tg])
                        pes[tg] = prog.inc("tensor", "s_pe", 1,
                                           lambda e, d=d, tg=tg, ws=ws: e.matmul(
                                               projb[tg][:, :], w_bf[ws][:, d, :],
                                               act_s[:, d, tg * 512:(tg + 1) * 512],
                                               start=(d == 0), stop=(d == 15)))
                buf_set(("wbf", ws), "s_pe", pes[3])
                if idx + 1 < len(order):
                    ws2, wrdy2 = load_weight_oc(w_r, order[idx + 1], 16)
                else:
                    ws2, wrdy2 = None, None
                for tg in range(NTG):
                    evac_fn(oc, tg, projbn[tg], projb[tg][:, :], pes[tg])
                if post_oc is not None:
                    post_oc(idx, oc)
                ws, wrdy = ws2, wrdy2

        def make_qkv_evac(mode):
            def evac(oc, tg, bank_name, psum_ap, pe_done):
                blk = oc // 8
                typ = "q" if blk in (0, 3) else ("k" if blk in (1, 4) else "v")
                i = CTR["qst"]
                CTR["qst"] += 1
                qs = i % 2
                if mode == "b":
                    buf_wait("vector", ("qst", qs))
                else:
                    buf_wait("scalar", ("qst", qs))
                    prog.wait("scalar", "s_pe", pe_done)
                if mode == "b":
                    prog.wait("vector", "s_pe", pe_done)
                    ac = prog.inc("vector", "s_dv", 1,
                                  lambda e, qs=qs, p=psum_ap, oc=oc: e.tensor_scalar_add(
                                      qstage[qs][:], p, bqkv_s[:, oc:oc + 1]))
                    acsem = "s_dv"
                else:
                    ac = prog.inc("scalar", "s_ac", 1,
                                  lambda e, qs=qs, p=psum_ap: e.activation(
                                      qstage[qs][:], p, AF.Copy))
                    acsem = "s_ac"
                bank_set(bank_name, acsem, ac)
                dv = None
                if mode == "a" and typ in ("q", "k"):
                    prog.wait("tensor", acsem, ac)
                    bank_wait("tensor", "pr")
                    pe2 = prog.inc("tensor", "s_pe", 1,
                                   lambda e, qs=qs: e.matmul(pr[:, :], rope_s[:], qstage[qs][:],
                                                             start=True, stop=True))
                    prog.wait("vector", acsem, ac) if acsem != "s_dv" else None
                    prog.wait("vector", "s_dv", prog.val("s_dv"))
                    prog.inc("vector", "s_dv", 1,
                             lambda e, qs=qs, tg=tg: e.tensor_tensor(
                                 rtmp[1][:], qstage[qs][:],
                                 cosb[:, tg * 512:(tg + 1) * 512], ALU.mult))
                    prog.wait("vector", "s_pe", pe2)
                    dvs = prog.inc("vector", "s_dv", 1,
                                   lambda e, tg=tg: e.tensor_tensor(
                                       rtmp[0][:], pr[:, :],
                                       sinb[:, tg * 512:(tg + 1) * 512], ALU.mult))
                    prog.wait("vector", "s_dv", dvs)
                    dv = prog.inc("vector", "s_dv", 1,
                                  lambda e, qs=qs: e.tensor_tensor(
                                      qstage[qs][:], rtmp[1][:], rtmp[0][:], ALU.add))
                    bank_set("pr", "s_dv", dv)
                prog.wait("gpsimd", acsem, ac)
                if dv:
                    prog.wait("gpsimd", "s_dv", dv)
                st = prog.inc("gpsimd", f"st_q{qs}", 16,
                              lambda e, qs=qs, oc=oc, tg=tg: e.dma_start(
                                  out=qkv_loc[oc * 128:(oc + 1) * 128, tg * 512:(tg + 1) * 512],
                                  in_=qstage[qs][:]))
                buf_set(("qst", qs), f"st_q{qs}", st)
            return evac

        def ph_qkv(mode, rhs_tg, st_barrier):
            w_r = wa_qkv_r if mode == "a" else wb_qkv_r
            evac = make_qkv_evac(mode)
            marks = {}
            wait_streams("gpsimd", ["ld_kfm", "ld_qg0", "ld_qg1", "ld_vc0", "ld_vc1"])
            order = list(range(24, 48)) + list(range(24))

            def post_oc(idx, oc):
                if idx == 23:
                    wait_streams("gpsimd", ["st_q0", "st_q1"])
                    marks["cc"] = prog.inc("gpsimd", "s_cc", 1,
                                           lambda e: e.collective_compute(
                                               "AllGather", ALU.bypass,
                                               replica_groups=GROUPS,
                                               ins=[qkv_loc[3072:6144, :]],
                                               outs=[g1[:, :]]))
            ph_proj(w_r, 48, evac, rhs_tg, oc_order=order, post_oc=post_oc)
            marks["pe_end"] = prog.val("s_pe")
            return marks

        def attn_load(eng_name, sem, dst_halves, mine_row, their_row):
            v = prog.bump(sem, 32)

            def do(e):
                par = get_par(e, "par_" + eng_name)
                with e.If_eq(par, 0):
                    e.dma_start(out=dst_halves[0],
                                in_=qkv_loc[mine_row:mine_row + 128, :]).then_inc(SEMS[sem], 16)
                    e.dma_start(out=dst_halves[1],
                                in_=g1[3072 + their_row:3072 + their_row + 128, :]).then_inc(SEMS[sem], 16)
                with e.Else():
                    e.dma_start(out=dst_halves[1],
                                in_=qkv_loc[mine_row:mine_row + 128, :]).then_inc(SEMS[sem], 16)
                    e.dma_start(out=dst_halves[0],
                                in_=g1[their_row:their_row + 128, :]).then_inc(SEMS[sem], 16)
            prog.raw(eng_name, do)
            return v

        def attn_load_cols(eng_name, sem, dst, row, c0, c1):
            v = prog.bump(sem, 16)
            lo = c0 < 2048

            def do(e):
                par = get_par(e, "par_" + eng_name)
                oc0 = c0 if lo else c0 - 2048
                oc1 = c1 if lo else c1 - 2048
                with e.If_eq(par, 0):
                    srcr = (qkv_loc if lo else g1.tensor)
                    if lo:
                        e.dma_start(out=dst, in_=qkv_loc[row:row + 128, oc0:oc1]).then_inc(SEMS[sem], 16)
                    else:
                        e.dma_start(out=dst, in_=g1[3072 + row:3072 + row + 128, oc0:oc1]).then_inc(SEMS[sem], 16)
                with e.Else():
                    if lo:
                        e.dma_start(out=dst, in_=g1[row:row + 128, oc0:oc1]).then_inc(SEMS[sem], 16)
                    else:
                        e.dma_start(out=dst, in_=qkv_loc[row:row + 128, oc0:oc1]).then_inc(SEMS[sem], 16)
            prog.raw(eng_name, do)
            return v

        def ph_attention(cc1, big2_pe_bar):
            wait_streams("sync", ["st_q0", "st_q1"])
            prog.wait("sync", "s_cc", cc1)
            prog.wait("scalar", "s_pe", big2_pe_bar)
            sband = [("pp0", pp[0]), ("pp1", pp[1]), ("pp2", pp[2])]
            last_S_pe = 0
            for h in range(HM):
                buf_wait("sync", ("kfm", 0))
                kld = attn_load("sync", "ld_kfm",
                                [kfm[:, 0:2048], kfm[:, 2048:4096]],
                                1024 + h * 128, 1024 + h * 128)
                for vb in range(8):
                    i = h * 8 + vb
                    sl = i % 2
                    buf_wait("sync", ("vch", sl))
                    vld = attn_load_cols("sync", f"ld_vc{sl}", vchunk[sl][:],
                                         2048 + h * 128, vb * 512, (vb + 1) * 512)
                    prog.wait("tensor", f"ld_vc{sl}", vld)
                    ph = vb % 2
                    bank_wait("tensor", f"ps_t{ph}")
                    buf_wait("tensor", ("vtm", 0))
                    for sub in range(4):
                        prog.inc("tensor", "s_pe", 1,
                                 lambda e, sl=sl, sub=sub, ph=ph: e.transpose(
                                     ps_tt[ph][:, sub * 128:(sub + 1) * 128],
                                     vchunk[sl][:, sub * 128:(sub + 1) * 128], identb_s[:]))
                    pe = prog.val("s_pe")
                    buf_set(("vch", sl), "s_pe", pe)
                    prog.wait("scalar", "s_pe", pe)
                    ac = prog.inc("scalar", "s_ac", 1,
                                  lambda e, vb=vb, ph=ph: e.copy(
                                      v_tm[:, vb * 512:(vb + 1) * 512], ps_tt[ph][:, :]))
                    bank_set(f"ps_t{ph}", "s_ac", ac)
                vt_ac = prog.val("s_ac")
                for qg in range(8):
                    qsl = qg % 2
                    buf_wait("sync", ("qg", qsl))
                    qld = attn_load_cols("sync", f"ld_qg{qsl}", qg_s[qsl][:],
                                         h * 128, qg * 512, (qg + 1) * 512)
                    for qt in range(4):
                        qa = 4 * qg + qt
                        P = 128 * (qa + 1)
                        nch = (P + 511) // 512
                        nsl = qt % 2
                        if P <= 2048:
                            ri = CTR["xstat"] % 2
                            CTR["xstat"] += 1
                            base = ri * 2048
                            regions = [f"scr{ri}"]
                        else:
                            base = 0
                            regions = ["scr0", "scr1"]
                        prog.wait("tensor", "ld_kfm", kld)
                        prog.wait("tensor", f"ld_qg{qsl}", qld)
                        for r in regions:
                            buf_wait("scalar", (r, 0))
                        buf_wait("vector", ("negm", nsl))
                        for kc in range(nch):
                            N = min(512, P - kc * 512)
                            bn, pb = sband[kc % 3]
                            bank_wait("tensor", bn)
                            pe = prog.inc("tensor", "s_pe", 1,
                                          lambda e, qsl=qsl, qt=qt, kc=kc, N=N, pb=pb: e.matmul(
                                              pb[:, 0:N],
                                              qg_s[qsl][:, qt * 128:(qt + 1) * 128],
                                              kfm[:, kc * 512:kc * 512 + N],
                                              start=True, stop=True))
                            last_S_pe = pe
                            prog.wait("vector", "s_pe", pe)
                            mdv = 0
                            if kc == nch - 1:
                                mdv = prog.inc("vector", "s_dv", 1,
                                               lambda e, N=N, pb=pb: e.tensor_tensor(
                                                   pb[:, N - 128:N], pb[:, N - 128:N],
                                                   mask_s[:], ALU.add))
                                prog.wait("vector", "s_dv", mdv)
                            prog.wait("scalar", "s_pe", pe)
                            if mdv:
                                prog.wait("scalar", "s_dv", mdv)
                            eva = prog.inc("scalar", "s_ac", 1,
                                           lambda e, kc=kc, N=N, pb=pb, base=base: e.copy(
                                               scr16[:, base + kc * 512:base + kc * 512 + N],
                                               pb[:, 0:N]))
                            prog.wait("vector", "s_ac", eva)
                            ev = prog.inc("vector", "s_dv", 1,
                                          lambda e, kc=kc, N=N, nsl=nsl, base=base: e.reduce_max(
                                              nparts[nsl][:, kc:kc + 1],
                                              scr16[:, base + kc * 512:base + kc * 512 + N],
                                              axis=AX.X, negate=True))
                            bank_set(bn, "s_ac", eva)
                        prog.wait("vector", "s_dv", prog.val("s_dv"))
                        dvk = prog.inc("vector", "s_dv", 1,
                                       lambda e, nch=nch, nsl=nsl: e.tensor_reduce(
                                           negm[nsl][:], nparts[nsl][:, 0:nch],
                                           axis=AX.X, op=ALU.min))
                        prog.wait("vector", "s_dv", dvk)
                        dv = prog.inc("vector", "s_dv", 1,
                                      lambda e, nsl=nsl: e.tensor_scalar_mul(
                                          negms[nsl][:], negm[nsl][:], SCL))
                        prog.wait("scalar", "s_dv", dv)
                        prog.wait("scalar", "s_ac", prog.val("s_ac"))
                        buf_wait("scalar", ("pslot", qt))
                        if qt == 0:
                            buf_wait("scalar", ("l4", 0))
                        ac2 = prog.inc("scalar", "s_ac", 1,
                                       lambda e, qt=qt, P=P, nsl=nsl, base=base: e.activation(
                                           big2[:, qt * 4096:qt * 4096 + P],
                                           scr16[:, base:base + P],
                                           AF.Exp, bias=negms[nsl][:], scale=sclv_s[:, :],
                                           accum_out=l4[:, qt:qt + 1]))
                        buf_set(("negm", nsl), "s_ac", ac2)
                        for r in regions:
                            buf_set((r, 0), "s_ac", ac2)
                            buf_add((r, 0), "s_dv", prog.val("s_dv"))
                        prog.wait("vector", "s_ac", ac2)
                        if qt == 0:
                            buf_wait("vector", ("rl4", 0))
                        prog.inc("vector", "s_dv", 1,
                                 lambda e, qt=qt: e.reciprocal(rl4[:, qt:qt + 1],
                                                               l4[:, qt:qt + 1]))
                    buf_set(("qg", qsl), "s_pe", last_S_pe)
                    dv_rl = prog.val("s_dv")
                    exp_ac = prog.val("s_ac")
                    nj = 4 * qg + 4
                    pv_pe = 0
                    pend_pv = None

                    def emit_T(j):
                        s0 = max(0, j - 4 * qg)
                        ph = j % 2
                        prog.wait("tensor", "s_ac", vt_ac)
                        prog.wait("tensor", "s_ac", exp_ac)
                        bank_wait("tensor", f"ps_t{ph}")
                        for qtl in range(s0, 4):
                            prog.inc("tensor", "s_pe", 1,
                                     lambda e, qtl=qtl, j=j, ph=ph: e.transpose(
                                         ps_tt[ph][:, qtl * 128:(qtl + 1) * 128],
                                         big2[:, qtl * 4096 + j * 128:qtl * 4096 + (j + 1) * 128],
                                         identb_s[:]))
                        pe_tr = prog.val("s_pe")
                        for qtl in range(s0, 4):
                            buf_add(("pslot", qtl), "s_pe", pe_tr)
                        prog.wait("scalar", "s_pe", pe_tr)
                        buf_wait("scalar", ("ptT", j % 2))
                        ac_pt = prog.inc("scalar", "s_ac", 1,
                                         lambda e, s0=s0, psl=j % 2, ph=ph: e.copy(
                                             ptT[psl][:, s0 * 128:512],
                                             ps_tt[ph][:, s0 * 128:512]))
                        bank_set(f"ps_t{ph}", "s_ac", ac_pt)
                        return (j, s0, ac_pt)

                    def emit_PV(t):
                        j, s0, ac_pt = t
                        prog.wait("tensor", "s_ac", ac_pt)
                        if j == 0:
                            bank_wait("tensor", "ppo")
                        pv = prog.inc("tensor", "s_pe", 1,
                                      lambda e, j=j, s0=s0, psl=j % 2, nj=nj: e.matmul(
                                          ppo[:, s0 * 128:512],
                                          v_tm[:, j * 128:(j + 1) * 128],
                                          ptT[psl][:, s0 * 128:512],
                                          start=(j == 0), stop=(j == nj - 1)))
                        buf_set(("ptT", j % 2), "s_pe", pv)
                        return pv

                    for j in range(nj):
                        t = emit_T(j)
                        if pend_pv is not None:
                            pv_pe = emit_PV(pend_pv)
                        pend_pv = t
                    pv_pe = emit_PV(pend_pv)
                    buf_set(("vtm", 0), "s_pe", pv_pe)
                    prog.wait("tensor", "s_dv", dv_rl)
                    bank_wait("tensor", "pr")
                    prog.inc("tensor", "s_pe", 1,
                             lambda e: e.transpose(pr[0:4, 0:128], rl4[:, :], identf_s[:]))
                    pe_t = prog.val("s_pe")
                    buf_set(("l4", 0), "s_pe", pe_t)
                    buf_set(("rl4", 0), "s_pe", pe_t)
                    prog.wait("vector", "s_pe", pe_t)
                    dv_t = prog.inc("vector", "s_dv", 1,
                                    lambda e: e.tensor_copy(rlT[:, :], pr[0:4, 0:128]))
                    prog.wait("tensor", "s_dv", dv_t)
                    for qt in range(4):
                        prog.inc("tensor", "s_pe", 1,
                                 lambda e, qt=qt: e.matmul(
                                     pr[:, qt * 128:(qt + 1) * 128],
                                     esel_s[:, qt * 128:(qt + 1) * 128], rlT[:, :],
                                     start=(qt == 0), stop=(qt == 3)))
                    rl_pe = prog.val("s_pe")
                    buf_wait("scalar", ("rlrep", 0))
                    prog.wait("scalar", "s_pe", rl_pe)
                    ac_rl = prog.inc("scalar", "s_ac", 1,
                                     lambda e: e.copy(rlrep[:, :], pr[:, :]))
                    bank_set("pr", "s_ac", ac_rl)
                    asl = qg % 2
                    buf_wait("vector", ("aost", asl))
                    prog.wait("vector", "s_pe", pv_pe)
                    prog.wait("vector", "s_ac", ac_rl)
                    ao_dv = prog.inc("vector", "s_dv", 1,
                                     lambda e, asl=asl: e.tensor_tensor(
                                         aost[asl][:], ppo[:, :], rlrep[:, :], ALU.mult))
                    bank_set("ppo", "s_dv", ao_dv)
                    buf_set(("rlrep", 0), "s_dv", ao_dv)
                    for qtl in range(4):
                        buf_set(("pslot", qtl), "s_pe", pv_pe)
                    prog.wait("gpsimd", "s_dv", ao_dv)
                    st = prog.inc("gpsimd", f"st_a{asl}", 16,
                                  lambda e, h=h, qg=qg, asl=asl: e.dma_start(
                                      out=ao_loc[h * 128:(h + 1) * 128, qg * 512:(qg + 1) * 512],
                                      in_=aost[asl][:]))
                    buf_set(("aost", asl), f"st_a{asl}", st)
                buf_set(("kfm", 0), "s_pe", last_S_pe)
                if h == 3:
                    wait_streams("gpsimd", ["st_a0", "st_a1"])
                    cc_half = prog.inc("gpsimd", "s_cc", 1,
                                       lambda e: e.collective_compute(
                                           "AllGather", ALU.bypass, replica_groups=GROUPS,
                                           ins=[ao_loc[0:512, :]], outs=[g2[0:1024, :]]))
            return {"pe": prog.val("s_pe")}

        def ph_exch2(attn_marks, qkv_pe_end):
            wait_streams("gpsimd", ["st_a0", "st_a1"])
            cc = prog.inc("gpsimd", "s_cc", 1,
                          lambda e: e.collective_compute(
                              "AllGather", ALU.bypass, replica_groups=GROUPS,
                              ins=[ao_loc[512:1024, :]], outs=[g2[1024:2048, :]]))
            prog.wait("gpsimd", "s_cc", cc)
            prog.wait("gpsimd", "s_pe", qkv_pe_end)

            def do(e):
                par = get_par(e, "par_gpsimd")
                with e.If_eq(par, 0):
                    e.dma_start(out=act_s[:, :, :],
                                in_=g2_r[:, :, 0:2048]).then_inc(SEMS["ld_ao"], 16)
                with e.Else():
                    e.dma_start(out=act_s[:, :, :],
                                in_=g2_r[:, :, 2048:4096]).then_inc(SEMS["ld_ao"], 16)
            prog.bump("ld_ao", 16)
            prog.raw("gpsimd", do)
            st = prog.val("ld_ao")
            prog.wait("tensor", "ld_ao", st)
            return st

        def ph_outproj(w_r, bias_tile, src_r, dst_r, fold_stats):
            def evac(oc, tg, bank_name, psum_ap, pe_done):
                bias_ap = bias_tile[:, oc:oc + 1] if bias_tile is not None else None
                resid_evac(oc, tg, bank_name, psum_ap, src_r, dst_r, bias_ap, None,
                           fold_stats, pe_done)
            ph_proj(w_r, 16, evac, [0, 0, 0, 0])
            return 0, prog.val("s_pe")

        def ph_gateup(rhs_tg):
            wait_streams("gpsimd", ["ld_hb"])

            def mm_block(ws, first):
                pes = [0] * 4
                for d in range(16):
                    for tg in range(NTG):
                        if d == 0:
                            if first:
                                prog.wait("tensor", "s_dv", rhs_tg[tg])
                            bank_wait("tensor", projbn[tg])
                        pes[tg] = prog.inc("tensor", "s_pe", 1,
                                           lambda e, d=d, tg=tg, ws=ws: e.matmul(
                                               projb[tg][:, :], w_bf[ws][:, d, :],
                                               act_s[:, d, tg * 512:(tg + 1) * 512],
                                               start=(d == 0), stop=(d == 15)))
                buf_set(("wbf", ws), "s_pe", pes[3])
                return pes

            ws, wrdy = load_weight_oc(w_gate_r, 0, 16)
            for ic in range(NIC):
                prog.wait("tensor", "s_pl", wrdy)
                pes = mm_block(ws, ic == 0)
                ws2, wrdy2 = load_weight_oc(w_up_r, ic, 16)
                g_acs = [0] * 4
                for tg in range(NTG):
                    prog.wait("scalar", "s_pe", pes[tg])
                    buf_wait("scalar", ("gst", tg))
                    g_acs[tg] = prog.inc("scalar", "s_ac", 1,
                                         lambda e, tg=tg, ic=ic: e.activation(
                                             gst[:, tg * 512:(tg + 1) * 512], projb[tg][:, :],
                                             AF.Silu, scale=sgate_s[:, ic:ic + 1]))
                    bank_set(projbn[tg], "s_ac", g_acs[tg])
                prog.wait("tensor", "s_pl", wrdy2)
                pes2 = mm_block(ws2, False)
                if ic + 1 < NIC:
                    ws, wrdy = load_weight_oc(w_gate_r, ic + 1, 16)
                for tg in range(NTG):
                    hi = CTR["hst"]
                    CTR["hst"] += 1
                    hsl = hi % 2
                    buf_wait("vector", ("hst", hsl))
                    prog.wait("vector", "s_pe", pes2[tg])
                    prog.wait("vector", "s_ac", g_acs[tg])
                    dv = prog.inc("vector", "s_dv", 1,
                                  lambda e, tg=tg, ic=ic, hsl=hsl: e.scalar_tensor_tensor(
                                      hst[hsl][:], projb[tg][:, :], sup_s[:, ic:ic + 1],
                                      gst[:, tg * 512:(tg + 1) * 512], ALU.mult, ALU.mult))
                    bank_set(projbn[tg], "s_dv", dv)
                    buf_set(("gst", tg), "s_dv", dv)
                    prog.wait("gpsimd", "s_dv", dv)
                    st = prog.inc("gpsimd", f"st_h{hsl}", 16,
                                  lambda e, ic=ic, tg=tg, hsl=hsl: e.dma_start(
                                      out=h_r[:, ic, tg * 512:(tg + 1) * 512], in_=hst[hsl][:]))
                    buf_set(("hst", hsl), f"st_h{hsl}", st)
            return 0, prog.val("s_pe")

        def ph_down(src_r, dst_r, fold_stats, big2_pe_bar, down_first):
            pend_holder = [None]
            WD = {}
            wait_streams("sync", ["st_h0", "st_h1"])
            prog.wait("sync", "s_pe", big2_pe_bar)
            for tg in range(NTG):
                if tg == 1 and down_first[0]:
                    wait_streams("sync", ["st_wd0", "st_wd1"])
                    down_first[0] = False
                buf_wait("sync", ("big2h", 0))
                hld = prog.inc("sync", "ld_hb", 16,
                               lambda e, tg=tg: e.dma_start(
                                   out=big2[:, 0:NIC * 512],
                                   in_=h_r[:, :, tg * 512:(tg + 1) * 512]))
                last_pe = 0

                def dsub(oc, sub, cache):
                    c0, c1 = sub * 8, min(NIC, sub * 8 + 8)
                    wi = CTR["wbf"]
                    CTR["wbf"] += 1
                    ws = wi % 2
                    if cache:
                        f = CTR["w32"]
                        CTR["w32"] += 1
                        fs = f % 2
                        buf_wait("sync", ("w32", fs))
                        ld = prog.inc("sync", f"ld_w{fs}", 16,
                                      lambda e, fs=fs, c0=c0, c1=c1, oc=oc: e.dma_start(
                                          out=w_f32[fs][:, 0:c1 - c0, :],
                                          in_=wdown_r[:, c0:c1, oc * 128:(oc + 1) * 128]))
                        buf_wait("gpsimd", ("wbf", ws))
                        prog.wait("gpsimd", f"ld_w{fs}", ld)
                        pl = prog.inc("gpsimd", "s_pl", 1,
                                      lambda e, fs=fs, ws=ws, c0=c0, c1=c1: e.tensor_copy(
                                          w_bf[ws][:, 0:c1 - c0, :], w_f32[fs][:, 0:c1 - c0, :]))
                        buf_set(("w32", fs), "s_pl", pl)
                        prog.wait("gpsimd", "s_pl", pl)
                        wd = prog.inc("gpsimd", f"st_wd{ws}", 16,
                                      lambda e, ws=ws, c0=c0, c1=c1, oc=oc: e.dma_start(
                                          out=wdownbf_r[:, c0:c1, oc * 128:(oc + 1) * 128],
                                          in_=w_bf[ws][:, 0:c1 - c0, :]))
                        WD[ws] = wd
                        return ws, "s_pl", pl, c0, c1
                    else:
                        c0, c1 = sub * 16, min(NIC, sub * 16 + 16)
                        wait_streams("sync", ["st_wd0", "st_wd1"])
                        buf_wait("sync", ("wbf", ws))
                        ld = prog.inc("sync", f"ld_b{ws}", 16,
                                      lambda e, ws=ws, c0=c0, c1=c1, oc=oc: e.dma_start(
                                          out=w_bf[ws][:, 0:c1 - c0, :],
                                          in_=wdownbf_r[:, c0:c1, oc * 128:(oc + 1) * 128]))
                        return ws, f"ld_b{ws}", ld, c0, c1

                cache = down_first[0] and tg == 0
                if tg == 0 and pend_holder[0] is None:
                    pend_holder[0] = dsub(0, 0, cache)
                nsub = 6 if cache else 3
                for oc in range(16):
                    bank = projbn[oc % 2]
                    pb = projb[oc % 2]
                    pe_done = 0
                    first = True
                    for sub in range(nsub):
                        ws, plsem, pl, c0, c1 = pend_holder[0]
                        prog.wait("tensor", plsem, pl)
                        if first:
                            prog.wait("tensor", "ld_hb", hld)
                            bank_wait("tensor", bank)
                        if sub + 1 < nsub:
                            pend_holder[0] = dsub(oc, sub + 1, cache)
                        elif oc + 1 < 16:
                            pend_holder[0] = dsub(oc + 1, 0, cache)
                        elif tg + 1 < NTG:
                            pend_holder[0] = dsub(0, 0, False)
                        else:
                            pend_holder[0] = None
                        for ici, ic in enumerate(range(c0, c1)):
                            pe_done = prog.inc("tensor", "s_pe", 1,
                                               lambda e, ws=ws, ici=ici, ic=ic, pb=pb,
                                               f=(first and ici == 0): e.matmul(
                                                   pb[:, :], w_bf[ws][:, ici, :],
                                                   big2[:, ic * 512:(ic + 1) * 512],
                                                   start=f, stop=(ic == NIC - 1)))
                        first = False
                        buf_set(("wbf", ws), "s_pe", pe_done)
                        if WD.get(ws) is not None:
                            buf_add(("wbf", ws), f"st_wd{ws}", WD.pop(ws))
                    last_pe = pe_done
                    resid_evac(oc, tg, bank, pb[:, :], src_r, dst_r, None,
                               sdown_s[:, oc:oc + 1], fold_stats, pe_done)
                buf_set(("big2h", 0), "s_pe", last_pe)
            return 0, prog.val("s_pe")

        down_first = [True]
        ph_const_loads()
        ph_ln_full(xin_r)

        st_bar = 0
        act_pe_bar = 0
        big2_pe_bar = 0
        attn_pe_end = 0

        sub_srcs = [xin_r, xstr_r, xstr_r, xstr_r]
        for sub in range(n_sub):
            dst_r = yout_r if sub == n_sub - 1 else xstr_r
            fold = sub < n_sub - 1
            src_r = sub_srcs[sub]
            stats_flush(0)
            rdy = ph_rstd()
            apl_tg = ph_apply(src_r, rdy, st_bar, act_pe_bar)
            if sub % 2 == 0:
                mode = "a" if sub == 0 else "b"
                marks = ph_qkv(mode, apl_tg, st_bar)
                if stop_after == "asm":
                    break
                am = ph_attention(marks["cc"], big2_pe_bar)
                attn_pe_end = am["pe"]
                big2_pe_bar = am["pe"]
                ph_exch2(am, marks["pe_end"])
                w_r = wa_proj_r if mode == "a" else wb_o_r
                btile = None if mode == "a" else bo_s
                STATS_INIT.clear()
                st_bar, act_pe_bar = ph_outproj(w_r, btile, src_r, dst_r, fold)
            else:
                gum_st, gum_pe = ph_gateup(apl_tg)
                act_pe_bar = gum_pe
                STATS_INIT.clear()
                st_bar, big2_pe_bar = ph_down(src_r, dst_r, fold, big2_pe_bar, down_first)

        ST_ALL = ["st_q0", "st_q1", "st_x0", "st_x1", "st_h0", "st_h1",
                  "st_a0", "st_a1", "st_asm"]
        if dbg:
            wait_streams("gpsimd", ST_ALL)
            prog.inc("gpsimd", "s_dbg", 16,
                     lambda e: e.dma_start(out=dbg_qkv, in_=qkvabs[:, :]))
            if stop_after is None:
                prog.inc("gpsimd", "s_dbg", 16,
                         lambda e: e.dma_start(out=dbg_ao, in_=g2[:, :]))
            if stop_after == "asm":
                prog.inc("gpsimd", "s_dbg", 16,
                         lambda e: e.dma_start(out=dbg_loc, in_=qkv_loc[:, :]))
                prog.inc("gpsimd", "s_dbg", 16,
                         lambda e: e.dma_start(out=dbg_xln_r, in_=act_s[:, :, :]))
            prog.wait("gpsimd", "s_dbg", prog.val("s_dbg"))
        wait_streams("gpsimd", ST_ALL)

        def replay(name):
            def f(e):
                for item in prog.ops[name]:
                    k = item[0]
                    if k == "wait":
                        e.wait_ge(SEMS[item[1]], item[2])
                    elif k == "inc":
                        item[3](e).then_inc(SEMS[item[1]], item[2])
                    else:
                        item[1](e)
            return f

        with nc.Block() as block:
            block.sync(replay("sync"))
            block.vector(replay("vector"))
            block.scalar(replay("scalar"))
            block.tensor(replay("tensor"))
            block.gpsimd(replay("gpsimd"))

    return nc


def _stage_core(inputs, core):
    p, e = core >> 1, core & 1
    f32 = np.float32
    x = np.asarray(inputs["L0_attn_a_x"], f32)
    xin = np.ascontiguousarray(x[p, e * TK:(e + 1) * TK, :].T)

    mine = list(range(8)) if e == 0 else list(range(8, 16))
    theirs = list(range(8, 16)) if e == 0 else list(range(8))

    def perm_cols(wT):
        qs = [wT[:, h * 128:(h + 1) * 128] for h in range(16)]
        ks = [wT[:, 2048 + h * 128:2048 + (h + 1) * 128] for h in range(16)]
        vs = [wT[:, 4096 + h * 128:4096 + (h + 1) * 128] for h in range(16)]
        blocks = ([qs[h] for h in mine] + [ks[h] for h in mine] + [vs[h] for h in mine]
                  + [qs[h] for h in theirs] + [ks[h] for h in theirs]
                  + [vs[h] for h in theirs])
        return np.ascontiguousarray(np.concatenate(blocks, axis=1))

    wa_qkv = perm_cols(np.asarray(inputs["L0_attn_a_w_qkv"], f32).T)
    gperm = [0, 1, 2, 3, 8, 9, 10, 11, 4, 5, 6, 7, 12, 13, 14, 15]

    def perm_rows(w):
        return np.ascontiguousarray(
            np.concatenate([w[h * 128:(h + 1) * 128, :] for h in gperm], axis=0))

    wa_proj = perm_rows(np.asarray(inputs["L0_attn_a_w_proj"], f32).T)

    Wq = np.asarray(inputs["L0_attn_b_Wq"], f32)
    Wk = np.asarray(inputs["L0_attn_b_Wk"], f32)
    Wv = np.asarray(inputs["L0_attn_b_Wv"], f32)
    wb_cat = np.concatenate([np.concatenate([Wq[h] for h in range(16)], axis=1),
                             np.concatenate([Wk[h] for h in range(16)], axis=1),
                             np.concatenate([Wv[h] for h in range(16)], axis=1)], axis=1)
    wb_qkv = perm_cols(wb_cat)
    bq = np.asarray(inputs["L0_attn_b_bq"], f32)
    bk = np.asarray(inputs["L0_attn_b_bk"], f32)
    bv = np.asarray(inputs["L0_attn_b_bv"], f32)
    b6 = np.concatenate([bq[h] for h in mine] + [bk[h] for h in mine]
                        + [bv[h] for h in mine] + [bq[h] for h in theirs]
                        + [bk[h] for h in theirs] + [bv[h] for h in theirs])
    wb_bias = np.ascontiguousarray(b6.reshape(48, 128).T)
    wb_o = perm_rows(np.asarray(inputs["L0_attn_b_Wo"], f32))
    wb_bo = np.ascontiguousarray(np.asarray(inputs["L0_attn_b_bo"], f32).reshape(16, 128).T)

    w_gate = np.ascontiguousarray(np.asarray(inputs["L0_ffn_gate_w"], f32).T)
    w_up = np.ascontiguousarray(np.asarray(inputs["L0_ffn_up_w"], f32).T)
    w_down = np.ascontiguousarray(np.asarray(inputs["L0_ffn_down_w"], f32).T)
    s_gate = np.ascontiguousarray(np.asarray(inputs["L0_ffn_gate_s"], f32).reshape(NIC, 128).T)
    s_up = np.ascontiguousarray(np.asarray(inputs["L0_ffn_up_s"], f32).reshape(NIC, 128).T)
    s_down = np.ascontiguousarray(np.asarray(inputs["L0_ffn_down_s"], f32).reshape(16, 128).T)

    pos = np.arange(e * TK, (e + 1) * TK, dtype=np.float64)
    inv = 10000.0 ** (-np.arange(0, 128, 2, dtype=np.float64) / 128.0)
    freqs = np.outer(pos, inv)
    emb = np.concatenate([freqs, freqs], axis=1)
    cos_in = np.ascontiguousarray(np.cos(emb).T.astype(f32))
    sin_in = np.ascontiguousarray(np.sin(emb).T.astype(f32))

    M = np.zeros((128, 128), f32)
    for d in range(64):
        M[d + 64, d] = -1.0
    for d in range(64, 128):
        M[d - 64, d] = 1.0

    esel = np.zeros((4, 512), f32)
    for qt in range(4):
        esel[qt, qt * 128:(qt + 1) * 128] = 1.0

    return {
        "xin": xin, "wa_qkv": wa_qkv, "wa_proj": wa_proj, "wb_qkv": wb_qkv,
        "wb_bias": wb_bias, "wb_o": wb_o, "wb_bo": wb_bo, "w_gate": w_gate,
        "w_up": w_up, "w_down": w_down, "s_gate": s_gate, "s_up": s_up,
        "s_down": s_down, "cos_in": cos_in, "sin_in": sin_in,
        "rope_in": M.astype(ml_dtypes.bfloat16),
        "identb_in": np.eye(128, dtype=ml_dtypes.bfloat16),
        "identf_in": np.eye(128, dtype=np.float32),
        "mask_in": np.triu(np.full((128, 128), NEG, f32), k=1),
        "onesc_in": np.full((128, 1), 1.0 / 2048.0, ml_dtypes.bfloat16),
        "onesr_in": np.ones((1, 128), f32),
        "esel_in": esel,
        "sclv_in": np.full((128, 1), SCL, f32),
    }


def kernel(**inputs):
    nc = build_program(N_SUB)
    in_maps = [_stage_core(inputs, c) for c in range(8)]
    res = run_bass_kernel_spmd(nc, in_maps, core_ids=list(range(8)))
    out = np.empty((4, T, D), np.float32)
    for c in range(8):
        p, e = c >> 1, c & 1
        out[p, e * TK:(e + 1) * TK, :] = res.results[c]["y_out"].T
    return out
